# version 02
# baseline (speedup 1.0000x reference)
import numpy as np
import ml_dtypes
from contextlib import ExitStack
import concourse.bass as bass
import concourse.mybir as mybir
from concourse.bass_utils import run_bass_kernel_spmd

F32 = mybir.dt.float32
BF16 = mybir.dt.bfloat16
I32 = mybir.dt.int32
AF = mybir.ActivationFunctionType
ALU = mybir.AluOpType
AX = mybir.AxisListType
BF = ml_dtypes.bfloat16

NCORES = 8
B, T, D = 2, 8192, 1024
NT = B * T
TPC = NT // NCORES
NBLK = TPC // 512
P = D // 128
H, DH = 16, 64
EPS = 1e-5
FJ = [64, 48, 32, 16]
NEXP = 8
HID = 2048
DFF = 4096
HID3 = 3072


class Sched:
    def __init__(self):
        self.steps = []
        self.t = 0

    def add(self, eng, fn, inc=1, kind='c', co=False):
        self.steps.append([eng, fn, self.t, inc, kind, co])
        self.t += inc

    def dma(self, out, in_, eng='sync', **kw):
        self.add(eng, (lambda e, o=out, i=in_, kw=kw: e.dma_start(out=o, in_=i, **kw)), 16, kind='dma')

    def op(self, eng, fn, co=False):
        self.add(eng, fn, 1, kind='c', co=co)


def emit(nc, s):
    skip = [False] * len(s.steps)
    for i in range(1, len(s.steps)):
        pe, _, _, _, pk, _ = s.steps[i - 1]
        ce, _, _, _, ck, cco = s.steps[i]
        if pe == ce and (ce == 'tensor' or (pk == 'dma' and ck == 'dma') or cco):
            skip[i] = True
    with nc.semaphore("tick") as TK:
        with nc.Block() as block:
            def run(eng_obj, name):
                for i, (engname, fn, tb, inc, kind, co) in enumerate(s.steps):
                    if engname == name:
                        if tb > 0 and not skip[i]:
                            eng_obj.wait_ge(TK, tb)
                        inst = fn(eng_obj)
                        inst.then_inc(TK, inc)
                if name == 'vector':
                    eng_obj.wait_ge(TK, s.t)

            @block.sync
            def _(e):
                run(e, 'sync')

            @block.scalar
            def _(e):
                run(e, 'scalar')

            @block.vector
            def _(e):
                run(e, 'vector')

            @block.tensor
            def _(e):
                run(e, 'tensor')

            @block.gpsimd
            def _(e):
                run(e, 'gpsimd')


def mm(s, out, lhsT, rhs, start, stop):
    s.op('tensor', lambda e, o=out, l=lhsT, r=rhs, st=start, sp=stop: e.matmul(o, l, r, start=st, stop=sp))


def ln_common(nc, s, es, W=512):
    c = {}
    c['ps'] = [es(nc.psum_tensor(f"psp{i}", [128, 512], F32)) for i in range(7)]
    c['psbf'] = es(nc.psum_tensor("pspbf", [128, 256], BF16))
    c['ones_col'] = es(nc.sbuf_tensor("ones_col", [128, 1], F32))
    c['ones_col_bf'] = es(nc.sbuf_tensor("ones_col_bf", [128, 1], BF16))
    c['ones_row'] = es(nc.sbuf_tensor("ones_row", [1, 128], F32))
    c['mu'] = es(nc.sbuf_tensor("mu", [1, W], F32))
    c['rs'] = es(nc.sbuf_tensor("rs", [1, W], F32))
    c['mub'] = es(nc.sbuf_tensor("mub", [128, W], F32))
    c['rstdb'] = es(nc.sbuf_tensor("rstdb", [128, W], F32))
    c['cen'] = [es(nc.sbuf_tensor(f"cenc{k}", [128, W], BF16)) for k in range(P)]
    c['sqc'] = es(nc.sbuf_tensor("sqc", [128, W], BF16))
    c['tmp'] = es(nc.sbuf_tensor("tmpf", [128, W], F32))
    c['epsb'] = es(nc.sbuf_tensor("epsb", [128, 1], F32))
    s.op('gpsimd', lambda e: e.memset(c['epsb'][:], EPS))
    s.op('gpsimd', lambda e: e.memset(c['ones_col'][:], 1.0))
    s.op('gpsimd', lambda e: e.memset(c['ones_col_bf'][:], 1.0))
    s.op('gpsimd', lambda e: e.memset(c['ones_row'][:], 1.0))
    return c


def ln_T(s, c, x_tiles, out_tiles, gain=None, bias=None):
    n = len(x_tiles)
    ps = c['ps']
    for k in range(n):
        mm(s, ps[0][0:1, :], c['ones_col'][:], x_tiles[k], k == 0, k == n - 1)
    s.op('scalar', lambda e: e.mul(out=c['mu'][0:1, :], in_=ps[0][0:1, :], mul=1.0 / D))
    mm(s, ps[1][:, :], c['ones_row'][0:1, :], c['mu'][0:1, :], True, True)
    s.op('scalar', lambda e: e.copy(out=c['mub'][:], in_=ps[1][:, :]))
    for k in range(n):
        s.op('vector', lambda e, k=k: e.tensor_tensor(out=c['cen'][k][:], in0=x_tiles[k], in1=c['mub'][:], op=ALU.subtract))
    for k in range(n):
        s.op('scalar', lambda e, k=k: e.activation(out=c['sqc'][:], in_=c['cen'][k][:], func=AF.Square))
        mm(s, ps[0][0:1, :], c['ones_col_bf'][:], c['sqc'][:], k == 0, k == n - 1)
    s.op('scalar', lambda e: e.activation(out=c['mu'][0:1, :], in_=ps[0][0:1, :], func=AF.Ln, bias=c['epsb'][0:1, :], scale=1.0 / D))
    s.op('scalar', lambda e: e.activation(out=c['rs'][0:1, :], in_=c['mu'][0:1, :], func=AF.Exp, scale=-0.5))
    mm(s, ps[1][:, :], c['ones_row'][0:1, :], c['rs'][0:1, :], True, True)
    s.op('scalar', lambda e: e.copy(out=c['rstdb'][:], in_=ps[1][:, :]))
    for k in range(n):
        if gain is None:
            s.op('vector', lambda e, k=k: e.tensor_tensor(out=out_tiles[k], in0=c['cen'][k][:], in1=c['rstdb'][:], op=ALU.mult))
        else:
            s.op('vector', lambda e, k=k: e.tensor_tensor(out=c['tmp'][:], in0=c['cen'][k][:], in1=c['rstdb'][:], op=ALU.mult))
            s.op('vector', lambda e, k=k, g=gain, b=bias: e.tensor_scalar(out=out_tiles[k], in0=c['tmp'][:], scalar1=g[:, k:k + 1], scalar2=b[:, k:k + 1], op0=ALU.mult, op1=ALU.add))


PREP_JOBS = [
    ("wqkv", 128, 3072, 'f32'), ("wout", 128, 1024, 'f32'),
    ("sw1", 128, 2048, 'f32'), ("sw2", 128, 2048, 'f32'), ("sw3", 256, 1024, 'f32'),
    ("ew1", 1024, 2048, 'f32'), ("ew2", 1024, 2048, 'f32'), ("ew3", 2048, 1024, 'f32'),
    ("rw", 128, 8, 'f32'),
    ("qkvw", 128, 3072, 'f32'), ("outw", 128, 1024, 'f32'),
    ("w1", 128, 4096, 'f32'), ("w2", 512, 1024, 'f32'),
    ("wgT", 128, 3072, 'i32'), ("wuT", 128, 3072, 'i32'), ("wdT", 384, 1024, 'i32'),
]


def build_prep():
    nc = bass.Bass("TRN2", target_bir_lowering=False)
    s = Sched()
    with ExitStack() as st:
        es = st.enter_context
        buf_f = es(nc.sbuf_tensor("buf_f", [128, 4096], F32))
        buf_i = es(nc.sbuf_tensor("buf_i", [128, 4096], I32))
        buf_b = es(nc.sbuf_tensor("buf_b", [128, 4096], BF16))
        for (name, rows, cols, sdt) in PREP_JOBS:
            src = nc.dram_tensor(name + "_in", [rows, cols], F32 if sdt == 'f32' else I32, kind="ExternalInput").ap()
            dst = nc.dram_tensor(name + "_out", [rows, cols], BF16, kind="ExternalOutput").ap()
            for r0 in range(0, rows, 128):
                if sdt == 'f32':
                    s.dma(buf_f[:, 0:cols], src[r0:r0 + 128, :])
                    s.op('scalar', lambda e, cols=cols: e.copy(out=buf_b[:, 0:cols], in_=buf_f[:, 0:cols]))
                else:
                    s.dma(buf_i[:, 0:cols], src[r0:r0 + 128, :])
                    s.op('vector', lambda e, cols=cols: e.tensor_copy(out=buf_f[:, 0:cols], in_=buf_i[:, 0:cols]))
                    s.op('scalar', lambda e, cols=cols: e.mul(out=buf_b[:, 0:cols], in_=buf_f[:, 0:cols], mul=1.0 / 127.0))
                s.dma(dst[r0:r0 + 128, :], buf_b[:, 0:cols])
        emit(nc, s)
    return nc


def build_A():
    nc = bass.Bass("TRN2", target_bir_lowering=False)
    s = Sched()
    x_in = nc.dram_tensor("x", [TPC, D], F32, kind="ExternalInput").ap()
    wq_in = nc.dram_tensor("wqkv", [D, 3 * D], BF16, kind="ExternalInput").ap()
    CS_in = {n: nc.dram_tensor(n, [128, TPC], F32, kind="ExternalInput").ap() for n in ['cq', 'sq', 'ck', 'sk']}
    id_in = nc.dram_tensor("ident", [128, 128], F32, kind="ExternalInput").ap()
    idb_in = nc.dram_tensor("identb", [128, 128], BF16, kind="ExternalInput").ap()
    xT_o = nc.dram_tensor("xT", [D, TPC], F32, kind="ExternalOutput").ap()
    qT_o = nc.dram_tensor("qT", [D, TPC], BF16, kind="ExternalOutput").ap()
    kT_o = nc.dram_tensor("kT", [D, TPC], BF16, kind="ExternalOutput").ap()
    v_o = nc.dram_tensor("v", [H, TPC, DH], BF16, kind="ExternalOutput").ap()
    with ExitStack() as st:
        es = st.enter_context
        ps = [es(nc.psum_tensor(f"psA{i}", [128, 512], F32)) for i in range(4)]
        psbf = es(nc.psum_tensor("psAbf", [128, 256], BF16))
        wq = es(nc.sbuf_tensor("wqA", [128, P * 3 * D], BF16))
        CS = {n: es(nc.sbuf_tensor(n + "_sb", [128, TPC], F32)) for n in ['cq', 'sq', 'ck', 'sk']}
        ident = es(nc.sbuf_tensor("identA", [128, 128], F32))
        identb = es(nc.sbuf_tensor("identbA", [128, 128], BF16))
        x_sb = es(nc.sbuf_tensor("x_sbA", [128, D], F32))
        stat = es(nc.sbuf_tensor("statA", [128, 1], F32))
        stat2 = es(nc.sbuf_tensor("stat2A", [128, 1], F32))
        cen = es(nc.sbuf_tensor("cenA", [128, D], F32))
        sqA = es(nc.sbuf_tensor("sqA", [128, D], F32))
        ln_bf = es(nc.sbuf_tensor("ln_bfA", [128, D], BF16))
        lnT = [es(nc.sbuf_tensor(f"lnTA{k}", [128, 512], BF16)) for k in range(P)]
        ev = es(nc.sbuf_tensor("evA", [128, 512], F32))
        ev2 = es(nc.sbuf_tensor("ev2A", [128, 512], F32))
        evs = es(nc.sbuf_tensor("evsA", [128, 512], F32))
        out_bf = es(nc.sbuf_tensor("obfA", [128, 512], BF16))
        epsA = es(nc.sbuf_tensor("epsA", [128, 1], F32))
        s.op('gpsimd', lambda e: e.memset(epsA[:], EPS))
        s.dma(ident[:], id_in[:])
        s.dma(identb[:], idb_in[:])
        for k in range(P):
            s.dma(wq[:, k * 3 * D:(k + 1) * 3 * D], wq_in[k * 128:(k + 1) * 128, :])
        for n in CS:
            s.dma(CS[n][:], CS_in[n][:])

        def wslice(k, col, width=128):
            return wq[:, k * 3 * D + col: k * 3 * D + col + width]

        for blk in range(NBLK):
            c0 = blk * 512
            for tt in range(4):
                t0 = (blk * 4 + tt) * 128
                s.dma(x_sb[:], x_in[t0:t0 + 128, :])
                s.op('vector', lambda e: e.reduce_sum(stat[:], x_sb[:], axis=AX.X))
                s.op('scalar', lambda e: e.mul(out=stat[:], in_=stat[:], mul=-1.0 / D))
                s.op('vector', lambda e: e.tensor_scalar(out=cen[:], in0=x_sb[:], scalar1=stat[:], scalar2=None, op0=ALU.add))
                s.op('scalar', lambda e: e.activation(out=sqA[:], in_=cen[:], func=AF.Square))
                s.op('vector', lambda e: e.reduce_sum(stat2[:], sqA[:], axis=AX.X))
                s.op('scalar', lambda e: e.activation(out=stat2[:], in_=stat2[:], func=AF.Ln, bias=epsA[:], scale=1.0 / D))
                s.op('scalar', lambda e: e.activation(out=stat2[:], in_=stat2[:], func=AF.Exp, scale=-0.5))
                s.op('vector', lambda e: e.tensor_scalar(out=ln_bf[:], in0=cen[:], scalar1=stat2[:], scalar2=None, op0=ALU.mult))
                for k in range(P):
                    s.op('tensor', lambda e, k=k: e.transpose(ps[2][:, 0:128], x_sb[:, k * 128:(k + 1) * 128], ident[:]))
                    s.op('scalar', lambda e: e.copy(out=ev[:, 0:128], in_=ps[2][:, 0:128]))
                    s.dma(xT_o[k * 128:(k + 1) * 128, t0:t0 + 128], ev[:, 0:128])
                    s.op('tensor', lambda e, k=k: e.transpose(psbf[:, 0:128], ln_bf[:, k * 128:(k + 1) * 128], identb[:]))
                    s.op('scalar', lambda e, k=k, tt=tt: e.copy(out=lnT[k][:, tt * 128:tt * 128 + 128], in_=psbf[:, 0:128]))
            for (col0, cs_c, cs_s, out_t) in [(0, 'cq', 'sq', qT_o), (D, 'ck', 'sk', kT_o)]:
                for m in range(P):
                    for k in range(P):
                        mm(s, ps[3][:, :], wslice(k, col0 + m * 128), lnT[k][:], k == 0, k == P - 1)
                    s.op('scalar', lambda e: e.copy(out=ev[:], in_=ps[3][:, :]))
                    for g in range(4):
                        a0, src0 = g * 32, (g ^ 1) * 32
                        s.dma(ev2[a0:a0 + 32, :], ev[src0:src0 + 32, :], eng='gpsimd')
                    s.op('vector', lambda e, c0=c0, cs=cs_c: e.tensor_tensor(out=evs[:], in0=ev[:], in1=CS[cs][:, c0:c0 + 512], op=ALU.mult))
                    s.op('vector', lambda e, c0=c0, cs=cs_s: e.tensor_tensor(out=ev2[:], in0=ev2[:], in1=CS[cs][:, c0:c0 + 512], op=ALU.mult))
                    s.op('vector', lambda e: e.tensor_tensor(out=out_bf[:], in0=evs[:], in1=ev2[:], op=ALU.add))
                    s.dma(out_t[m * 128:(m + 1) * 128, c0:c0 + 512], out_bf[:])
            for tt in range(4):
                t0 = (blk * 4 + tt) * 128
                for n2 in range(2):
                    for k in range(P):
                        mm(s, ps[1][:, :], lnT[k][:, tt * 128:tt * 128 + 128], wslice(k, 2 * D + n2 * 512, 512), k == 0, k == P - 1)
                    s.op('scalar', lambda e: e.copy(out=out_bf[:], in_=ps[1][:, :]))
                    for hh in range(8):
                        h = n2 * 8 + hh
                        s.dma(v_o[h, t0:t0 + 128, :], out_bf[:, hh * 64:hh * 64 + 64])
        emit(nc, s)
    return nc


def build_B():
    nc = bass.Bass("TRN2", target_bir_lowering=False)
    s = Sched()
    qT_in = nc.dram_tensor("qT", [D, 4 * 512], BF16, kind="ExternalInput").ap()
    kT_in = [nc.dram_tensor(f"kT{j}", [D, FJ[j] * 128], BF16, kind="ExternalInput").ap() for j in range(4)]
    v_in = [nc.dram_tensor(f"v{j}", [H, 128, FJ[j] * 65], BF16, kind="ExternalInput").ap() for j in range(4)]
    mask_in = nc.dram_tensor("masks", [128, sum(FJ) * 512], BF16, kind="ExternalInput").ap()
    x0_in = nc.dram_tensor("x0", [D, 4 * 512], F32, kind="ExternalInput").ap()
    wo_in = nc.dram_tensor("wout", [D, D], BF16, kind="ExternalInput").ap()
    xT1_o = nc.dram_tensor("xT1", [D, 4 * 512], F32, kind="ExternalOutput").ap()
    with ExitStack() as st:
        es = st.enter_context
        ps = [es(nc.psum_tensor(f"psB{i}", [128, 512], F32)) for i in range(7)]
        onesr = es(nc.sbuf_tensor("onesrB", [1, 128], F32))
        wo = es(nc.sbuf_tensor("woB", [128, P * D], BF16))
        mask_sb = es(nc.sbuf_tensor("maskB", [128, 64 * 512], BF16))
        k_sb = es(nc.sbuf_tensor("k_sbB", [64, 8192], BF16))
        q_sb = es(nc.sbuf_tensor("q_sbB", [64, 512], BF16))
        v_sb = es(nc.sbuf_tensor("v_sbB", [128, 64 * 65], BF16))
        p2_sb4 = [es(nc.sbuf_tensor(f"p2_sbB{i}", [128, 512], BF16)) for i in range(4)]
        rec = es(nc.sbuf_tensor("recB", [1, 512], F32))
        num_sb = es(nc.sbuf_tensor("numB", [64, 512], F32))
        yT = [es(nc.sbuf_tensor(f"yTB{k}", [128, 512], BF16)) for k in range(P)]
        x0_sb = es(nc.sbuf_tensor("x0B", [128, 512], F32))
        out_sb = es(nc.sbuf_tensor("outB", [128, 512], F32))
        s.op('gpsimd', lambda e: e.memset(onesr[:], 1.0))
        for k in range(P):
            s.dma(wo[:, k * D:(k + 1) * D], wo_in[k * 128:(k + 1) * 128, :])
        moff = 0
        for j in range(4):
            F = FJ[j]
            w = F * 512 // 2
            for half in range(2):
                s.dma(mask_sb[:, half * w:(half + 1) * w], mask_in[:, moff + half * w: moff + (half + 1) * w])
            moff += F * 512
            for h in range(H):
                s.dma(k_sb[:, 0:F * 128], kT_in[j][h * 64:h * 64 + 64, :])
                s.dma(q_sb[:], qT_in[h * 64:h * 64 + 64, j * 512:(j + 1) * 512])
                s.dma(v_sb[:, 0:F * 65], v_in[j][h, :, :])
                for n0 in range(0, F, 4):
                    for u in range(4):
                        n = n0 + u
                        mm(s, ps[2 + u][:, :], k_sb[0:64, n * 128:(n + 1) * 128], q_sb[:], True, True)
                    for u in range(4):
                        n = n0 + u
                        s.op('scalar', lambda e, u=u, n=n: e.activation(out=p2_sb4[u][:], in_=ps[2 + u][:, :], func=AF.Exp), co=(u > 0))
                    for u in range(4):
                        n = n0 + u
                        s.op('vector', lambda e, u=u, n=n: e.tensor_tensor(out=p2_sb4[u][:], in0=p2_sb4[u][:], in1=mask_sb[:, n * 512:(n + 1) * 512], op=ALU.mult), co=(u > 0))
                    for u in range(4):
                        n = n0 + u
                        mm(s, ps[1][0:65, :], v_sb[:, n * 65:n * 65 + 65], p2_sb4[u][:], n == 0, n == F - 1)
                s.op('vector', lambda e: e.reciprocal(rec[:], ps[1][64:65, :]))
                mm(s, ps[2][0:64, :], onesr[0:1, 0:64], rec[0:1, :], True, True)
                s.op('scalar', lambda e: e.copy(out=num_sb[:], in_=ps[1][0:64, :]))
                s.op('vector', lambda e, h=h: e.tensor_tensor(out=yT[h // 2][(h % 2) * 64:(h % 2) * 64 + 64, :], in0=num_sb[:], in1=ps[2][0:64, :], op=ALU.mult))
            for m in range(P):
                for k in range(P):
                    mm(s, ps[3][:, :], wo[:, k * D + m * 128: k * D + m * 128 + 128], yT[k][:], k == 0, k == P - 1)
                s.dma(x0_sb[:], x0_in[m * 128:(m + 1) * 128, j * 512:(j + 1) * 512])
                s.op('vector', lambda e: e.tensor_tensor(out=out_sb[:], in0=x0_sb[:], in1=ps[3][:, :], op=ALU.add))
                s.dma(xT1_o[m * 128:(m + 1) * 128, j * 512:(j + 1) * 512], out_sb[:])
        emit(nc, s)
    return nc


def build_C():
    nc = bass.Bass("TRN2", target_bir_lowering=False)
    s = Sched()
    x_in = nc.dram_tensor("x1", [D, TPC], F32, kind="ExternalInput").ap()
    sw1_in = nc.dram_tensor("sw1", [D, HID], BF16, kind="ExternalInput").ap()
    sw2_in = nc.dram_tensor("sw2", [D, HID], BF16, kind="ExternalInput").ap()
    sw3_in = nc.dram_tensor("sw3", [HID, D], BF16, kind="ExternalInput").ap()
    ew1_in = nc.dram_tensor("ew1", [NEXP * D, HID], BF16, kind="ExternalInput").ap()
    ew2_in = nc.dram_tensor("ew2", [NEXP * D, HID], BF16, kind="ExternalInput").ap()
    ew3_in = nc.dram_tensor("ew3", [NEXP * HID, D], BF16, kind="ExternalInput").ap()
    rw_in = nc.dram_tensor("rw", [D, NEXP], BF16, kind="ExternalInput").ap()
    rb_in = nc.dram_tensor("rb", [NEXP], F32, kind="ExternalInput").ap()
    id_in = nc.dram_tensor("ident", [128, 128], F32, kind="ExternalInput").ap()
    x2_o = nc.dram_tensor("x2", [D, TPC], F32, kind="ExternalOutput").ap()
    gates_d = nc.dram_tensor("gates_scr", [NBLK * NEXP, 512], F32, kind="Internal").ap()
    NH = HID // 128
    with ExitStack() as st:
        es = st.enter_context
        c = ln_common(nc, s, es)
        ps = c['ps']
        wbuf = es(nc.sbuf_tensor("wbuf", [128, 49152], BF16))
        ln_bf = [[es(nc.sbuf_tensor(f"lnbf{b}_{k}", [128, 512], BF16)) for k in range(P)] for b in range(NBLK)]
        gt_sb = es(nc.sbuf_tensor("gt_sb", [8, 512], F32))
        grow = es(nc.sbuf_tensor("grow", [1, 512], F32))
        ident = es(nc.sbuf_tensor("identC", [128, 128], F32))
        rb_sb = es(nc.sbuf_tensor("rb_sb", [8, 1], F32))
        rw_sb = es(nc.sbuf_tensor("rw_sb", [128, P * 8], BF16))
        x_t = [es(nc.sbuf_tensor(f"xtC{k}", [128, 512], F32)) for k in range(P)]
        h1 = es(nc.sbuf_tensor("h1C", [128, 512], BF16))
        h_bf = [es(nc.sbuf_tensor(f"hbfC{k}", [128, 512], BF16)) for k in range(NH)]
        pm = es(nc.sbuf_tensor("pmC", [128, 8], F32))
        pmax = es(nc.sbuf_tensor("pmaxC", [128, 1], F32))
        m2 = es(nc.sbuf_tensor("m2C", [128, 1], F32))
        pm2 = es(nc.sbuf_tensor("pm2C", [128, 8], F32))
        gsc = es(nc.sbuf_tensor("gscC", [128, 512], BF16))
        ev = es(nc.sbuf_tensor("evC", [128, 512], F32))
        s.dma(ident[:], id_in[:])
        s.dma(rb_sb[:], rb_in[:])
        for k in range(P):
            s.dma(wbuf[:, k * HID:(k + 1) * HID], sw1_in[k * 128:(k + 1) * 128, :])
            s.dma(wbuf[:, 16384 + k * HID:16384 + (k + 1) * HID], sw2_in[k * 128:(k + 1) * 128, :])
            s.dma(rw_sb[:, k * 8:(k + 1) * 8], rw_in[k * 128:(k + 1) * 128, :])
        for k in range(NH):
            s.dma(wbuf[:, 32768 + k * D:32768 + (k + 1) * D], sw3_in[k * 128:(k + 1) * 128, :])
        for blk in range(NBLK):
            b0 = blk * 512
            for k in range(P):
                s.dma(x_t[k][:], x_in[k * 128:(k + 1) * 128, b0:b0 + 512])
            ln_T(s, c, [x_t[k][:] for k in range(P)], [ln_bf[blk][k][:] for k in range(P)])
            for k in range(P):
                mm(s, ps[2][0:8, :], rw_sb[:, k * 8:(k + 1) * 8], ln_bf[blk][k][:], k == 0, k == P - 1)
            s.op('vector', lambda e: e.tensor_scalar(out=ev[0:8, :], in0=ps[2][0:8, :], scalar1=rb_sb[:], scalar2=None, op0=ALU.add))
            for tt in range(4):
                s.op('tensor', lambda e, tt=tt: e.transpose(ps[3][:, 0:8], ev[0:8, tt * 128:tt * 128 + 128], ident[0:8, 0:8]))
                s.op('scalar', lambda e: e.copy(out=pm[:], in_=ps[3][:, 0:8]))
                s.op('vector', lambda e: e.reduce_max(pmax[:], pm[:], axis=AX.X, negate=True))
                s.op('scalar', lambda e: e.activation(out=pm[:], in_=pm[:], func=AF.Exp, bias=pmax[:], scale=1.0))
                s.op('vector', lambda e: e.reduce_max(pmax[:], pm[:], axis=AX.X))
                s.op('vector', lambda e: e.tensor_scalar(out=pm2[:], in0=pm[:], scalar1=pmax[:], scalar2=-1e30, op0=ALU.is_ge, op1=ALU.mult))
                s.op('vector', lambda e: e.tensor_tensor(out=pm2[:], in0=pm[:], in1=pm2[:], op=ALU.add))
                s.op('vector', lambda e: e.reduce_max(m2[:], pm2[:], axis=AX.X))
                s.op('vector', lambda e: e.tensor_tensor(out=pmax[:], in0=pmax[:], in1=m2[:], op=ALU.add))
                s.op('vector', lambda e: e.reciprocal(pmax[:], pmax[:]))
                s.op('vector', lambda e: e.tensor_scalar(out=pm2[:], in0=pm[:], scalar1=m2[:], scalar2=None, op0=ALU.is_ge))
                s.op('vector', lambda e: e.tensor_tensor(out=pm2[:], in0=pm[:], in1=pm2[:], op=ALU.mult))
                s.op('vector', lambda e: e.tensor_scalar(out=pm2[:], in0=pm2[:], scalar1=pmax[:], scalar2=None, op0=ALU.mult))
                s.op('tensor', lambda e: e.transpose(ps[3][0:8, 0:128], pm2[:], ident[:]))
                s.op('scalar', lambda e, tt=tt: e.copy(out=gt_sb[:, tt * 128:tt * 128 + 128], in_=ps[3][0:8, 0:128]))
            s.dma(gates_d[blk * 8:(blk + 1) * 8, :], gt_sb[:], eng='gpsimd')
            for m in range(NH):
                for k in range(P):
                    mm(s, ps[2][:, :], wbuf[:, k * HID + m * 128: k * HID + m * 128 + 128], ln_bf[blk][k][:], k == 0, k == P - 1)
                s.op('scalar', lambda e: e.activation(out=h1[:], in_=ps[2][:, :], func=AF.Silu))
                for k in range(P):
                    mm(s, ps[3][:, :], wbuf[:, 16384 + k * HID + m * 128: 16384 + k * HID + m * 128 + 128], ln_bf[blk][k][:], k == 0, k == P - 1)
                s.op('vector', lambda e, m=m: e.tensor_tensor(out=h_bf[m][:], in0=h1[:], in1=ps[3][:, :], op=ALU.mult))
            for m in range(P):
                for k in range(NH):
                    mm(s, ps[2][:, :], wbuf[:, 32768 + k * D + m * 128: 32768 + k * D + m * 128 + 128], h_bf[k][:], k == 0, k == NH - 1)
                s.op('vector', lambda e, m=m: e.tensor_tensor(out=x_t[m][:], in0=x_t[m][:], in1=ps[2][:, :], op=ALU.add))
                s.dma(x2_o[m * 128:(m + 1) * 128, b0:b0 + 512], x_t[m][:])
        for ex in range(NEXP):
            for k in range(P):
                s.dma(wbuf[:, k * HID:(k + 1) * HID], ew1_in[ex * D + k * 128:ex * D + (k + 1) * 128, :])
                s.dma(wbuf[:, 16384 + k * HID:16384 + (k + 1) * HID], ew2_in[ex * D + k * 128:ex * D + (k + 1) * 128, :])
            for k in range(NH):
                s.dma(wbuf[:, 32768 + k * D:32768 + (k + 1) * D], ew3_in[ex * HID + k * 128:ex * HID + (k + 1) * 128, :])
            for blk in range(NBLK):
                b0 = blk * 512
                for m in range(NH):
                    for k in range(P):
                        mm(s, ps[2][:, :], wbuf[:, k * HID + m * 128: k * HID + m * 128 + 128], ln_bf[blk][k][:], k == 0, k == P - 1)
                    s.op('scalar', lambda e: e.activation(out=h1[:], in_=ps[2][:, :], func=AF.Silu))
                    for k in range(P):
                        mm(s, ps[3][:, :], wbuf[:, 16384 + k * HID + m * 128: 16384 + k * HID + m * 128 + 128], ln_bf[blk][k][:], k == 0, k == P - 1)
                    s.op('vector', lambda e, m=m: e.tensor_tensor(out=h_bf[m][:], in0=h1[:], in1=ps[3][:, :], op=ALU.mult))
                s.dma(grow[:], gates_d[blk * 8 + ex:blk * 8 + ex + 1, :])
                mm(s, ps[4][:, :], c['ones_row'][0:1, :], grow[0:1, :], True, True)
                s.op('scalar', lambda e: e.copy(out=gsc[:], in_=ps[4][:, :]))
                for m in range(NH):
                    s.op('vector', lambda e, m=m: e.tensor_tensor(out=h_bf[m][:], in0=h_bf[m][:], in1=gsc[:], op=ALU.mult))
                for m in range(P):
                    for k in range(NH):
                        mm(s, ps[2][:, :], wbuf[:, 32768 + k * D + m * 128: 32768 + k * D + m * 128 + 128], h_bf[k][:], k == 0, k == NH - 1)
                    s.op('scalar', lambda e: e.copy(out=ev[:], in_=ps[2][:, :]))
                    s.dma(x2_o[m * 128:(m + 1) * 128, b0:b0 + 512], ev[:], eng='gpsimd', accum_op=ALU.add)
        emit(nc, s)
    return nc


def build_D():
    nc = bass.Bass("TRN2", target_bir_lowering=False)
    s = Sched()
    x_in = nc.dram_tensor("x2", [D, TPC], F32, kind="ExternalInput").ap()
    wq_in = nc.dram_tensor("qkvw", [D, 3 * D], BF16, kind="ExternalInput").ap()
    qb_in = nc.dram_tensor("qkvb", [3 * D], F32, kind="ExternalInput").ap()
    g1_in = nc.dram_tensor("g1", [D], F32, kind="ExternalInput").ap()
    be1_in = nc.dram_tensor("be1", [D], F32, kind="ExternalInput").ap()
    qfT_o = nc.dram_tensor("qfT", [D, TPC], BF16, kind="ExternalOutput").ap()
    kvz_o = nc.dram_tensor("kvz", [128, 520], F32, kind="ExternalOutput").ap()
    with ExitStack() as st:
        es = st.enter_context
        c = ln_common(nc, s, es)
        ps = c['ps']
        wq = es(nc.sbuf_tensor("wqD", [128, P * 3 * D], BF16))
        x_t = [es(nc.sbuf_tensor(f"xtD{k}", [128, 512], F32)) for k in range(P)]
        src_t = [es(nc.sbuf_tensor(f"srcD{k}", [128, 512], F32)) for k in range(P)]
        ln1 = [es(nc.sbuf_tensor(f"ln1D{k}", [128, 512], BF16)) for k in range(P)]
        gb = es(nc.sbuf_tensor("gbD", [128, 2 * P], F32))
        qb_q = es(nc.sbuf_tensor("qbqD", [128, P], F32))
        qrow = es(nc.sbuf_tensor("qrowD", [1, 3 * D], F32))
        bias_b = es(nc.sbuf_tensor("biasbD", [128, 2 * D], F32))
        kf_sb = es(nc.sbuf_tensor("kfD", [128, 16 * D], BF16))
        v_sb = es(nc.sbuf_tensor("vD", [128, 16 * D], BF16))
        onesb = es(nc.sbuf_tensor("onesbD", [128, 1], BF16))
        ev = es(nc.sbuf_tensor("evD", [128, 512], F32))
        ev2 = es(nc.sbuf_tensor("ev2D", [128, 512], F32))
        ev3 = es(nc.sbuf_tensor("ev3D", [128, 512], F32))
        out_bf = es(nc.sbuf_tensor("obfD", [128, 512], BF16))
        kvz_sb = es(nc.sbuf_tensor("kvzD", [128, 520], F32))
        s.op('gpsimd', lambda e: e.memset(onesb[:], 1.0))
        for k in range(P):
            s.dma(wq[:, k * 3 * D:(k + 1) * 3 * D], wq_in[k * 128:(k + 1) * 128, :])
            s.dma(gb[:, k:k + 1], g1_in[k * 128:(k + 1) * 128])
            s.dma(gb[:, P + k:P + k + 1], be1_in[k * 128:(k + 1) * 128])
            s.dma(qb_q[:, k:k + 1], qb_in[k * 128:(k + 1) * 128])
        s.dma(qrow[:], qb_in[:])
        for n4 in range(4):
            mm(s, ps[2][:, :], c['ones_row'][0:1, :], qrow[0:1, D + n4 * 512:D + (n4 + 1) * 512], True, True)
            s.op('scalar', lambda e, n4=n4: e.copy(out=bias_b[:, n4 * 512:(n4 + 1) * 512], in_=ps[2][:, :]))

        def eluplus(src_ap, dst_ap):
            s.op('scalar', lambda e, a=src_ap: e.activation(out=ev2[:], in_=a, func=AF.Relu))
            s.op('vector', lambda e, a=src_ap: e.tensor_scalar(out=ev3[:], in0=a, scalar1=0.0, scalar2=None, op0=ALU.min))
            s.op('scalar', lambda e: e.activation(out=ev3[:], in_=ev3[:], func=AF.Exp))
            s.op('vector', lambda e, d=dst_ap: e.tensor_tensor(out=d, in0=ev2[:], in1=ev3[:], op=ALU.add))

        for blk in range(NBLK):
            b0 = blk * 512
            for k in range(P):
                s.dma(x_t[k][:], x_in[k * 128:(k + 1) * 128, b0:b0 + 512])
            ln_T(s, c, [x_t[k][:] for k in range(P)], [src_t[k][:] for k in range(P)])
            ln_T(s, c, [src_t[k][:] for k in range(P)], [ln1[k][:] for k in range(P)], gain=gb[:, 0:P], bias=gb[:, P:2 * P])
            for m in range(P):
                for k in range(P):
                    mm(s, ps[2][:, :], wq[:, k * 3 * D + m * 128: k * 3 * D + m * 128 + 128], ln1[k][:], k == 0, k == P - 1)
                s.op('vector', lambda e, m=m: e.tensor_scalar(out=ev[:], in0=ps[2][:, :], scalar1=qb_q[:, m:m + 1], scalar2=None, op0=ALU.add))
                eluplus(ev[:], out_bf[:])
                s.dma(qfT_o[m * 128:(m + 1) * 128, b0:b0 + 512], out_bf[:])
            for tt in range(4):
                ti = blk * 4 + tt
                for half in range(2):
                    for n2 in range(2):
                        col0 = D + half * D + n2 * 512
                        for k in range(P):
                            mm(s, ps[3][:, :], ln1[k][:, tt * 128:tt * 128 + 128], wq[:, k * 3 * D + col0: k * 3 * D + col0 + 512], k == 0, k == P - 1)
                        s.op('vector', lambda e, half=half, n2=n2: e.tensor_tensor(out=ev[:], in0=ps[3][:, :], in1=bias_b[:, half * D + n2 * 512: half * D + (n2 + 1) * 512], op=ALU.add))
                        if half == 0:
                            eluplus(ev[:], kf_sb[:, ti * D + n2 * 512: ti * D + (n2 + 1) * 512])
                        else:
                            s.op('vector', lambda e, ti=ti, n2=n2: e.tensor_copy(out=v_sb[:, ti * D + n2 * 512: ti * D + (n2 + 1) * 512], in_=ev[:]))
        for hp in range(8):
            for sub in range(2):
                h = 2 * hp + sub
                pbank = ps[4 + sub]
                for ti in range(16):
                    mm(s, pbank[0:64, 0:64], kf_sb[:, ti * D + h * 64: ti * D + h * 64 + 64], v_sb[:, ti * D + h * 64: ti * D + h * 64 + 64], ti == 0, False)
                    mm(s, pbank[0:64, 64:65], kf_sb[:, ti * D + h * 64: ti * D + h * 64 + 64], onesb[:], ti == 0, ti == 15)
                s.op('scalar', lambda e, hp=hp, sub=sub, pb=pbank: e.copy(out=kvz_sb[sub * 64:sub * 64 + 64, hp * 65:(hp + 1) * 65], in_=pb[0:64, 0:65]))
        s.dma(kvz_o[:], kvz_sb[:])
        emit(nc, s)
    return nc


def build_E():
    nc = bass.Bass("TRN2", target_bir_lowering=False)
    s = Sched()
    x2_in = nc.dram_tensor("x2", [D, TPC], F32, kind="ExternalInput").ap()
    qfT_in = nc.dram_tensor("qfT", [D, TPC], BF16, kind="ExternalInput").ap()
    kvz_in = nc.dram_tensor("kvzp", [4, 128, 520], F32, kind="ExternalInput").ap()
    wo_in = nc.dram_tensor("outw", [D, D], BF16, kind="ExternalInput").ap()
    ob_in = nc.dram_tensor("outb", [D], F32, kind="ExternalInput").ap()
    w1_in = nc.dram_tensor("w1", [D, DFF], BF16, kind="ExternalInput").ap()
    b1_in = nc.dram_tensor("b1", [DFF], F32, kind="ExternalInput").ap()
    w2_in = nc.dram_tensor("w2", [DFF, D], BF16, kind="ExternalInput").ap()
    b2_in = nc.dram_tensor("b2", [D], F32, kind="ExternalInput").ap()
    g2_in = nc.dram_tensor("g2", [D], F32, kind="ExternalInput").ap()
    be2_in = nc.dram_tensor("be2", [D], F32, kind="ExternalInput").ap()
    x4_o = nc.dram_tensor("x4", [D, TPC], F32, kind="ExternalOutput").ap()
    NF = DFF // 128
    with ExitStack() as st:
        es = st.enter_context
        c = ln_common(nc, s, es)
        ps = c['ps']
        wo = es(nc.sbuf_tensor("woE", [128, P * D], BF16))
        w1 = es(nc.sbuf_tensor("w1E", [128, P * DFF], BF16))
        w2m = es(nc.sbuf_tensor("w2mE", [128, NF * 128], BF16))
        kvz = es(nc.sbuf_tensor("kvzE", [128, 520], F32))
        kvzb = es(nc.sbuf_tensor("kvzbE", [128, 520], BF16))
        kvp = es(nc.sbuf_tensor("kvpE", [128, 520], F32))
        gb = es(nc.sbuf_tensor("gbE", [128, 2 * P], F32))
        ob_sb = es(nc.sbuf_tensor("obE", [128, P], F32))
        b2_sb = es(nc.sbuf_tensor("b2E", [128, P], F32))
        b1_sb = es(nc.sbuf_tensor("b1E", [128, NF], F32))
        qf_t = [es(nc.sbuf_tensor(f"qfE{k}", [128, 512], BF16)) for k in range(P)]
        x2_t = [es(nc.sbuf_tensor(f"x2E{k}", [128, 512], F32)) for k in range(P)]
        xi_t = [es(nc.sbuf_tensor(f"xiE{k}", [128, 512], F32)) for k in range(P)]
        ln2 = [es(nc.sbuf_tensor(f"ln2E{k}", [128, 512], BF16)) for k in range(P)]
        h_bf = [es(nc.sbuf_tensor(f"hbfE{k}", [128, 512], BF16)) for k in range(NF)]
        rec = es(nc.sbuf_tensor("recE", [1, 512], F32))
        num_sb = es(nc.sbuf_tensor("numE", [64, 512], F32))
        yT = [es(nc.sbuf_tensor(f"yTE{k}", [128, 512], BF16)) for k in range(P)]
        ev = es(nc.sbuf_tensor("evE", [128, 512], F32))
        for k in range(P):
            s.dma(wo[:, k * D:(k + 1) * D], wo_in[k * 128:(k + 1) * 128, :])
            s.dma(w1[:, k * DFF:(k + 1) * DFF], w1_in[k * 128:(k + 1) * 128, :])
            s.dma(gb[:, k:k + 1], g2_in[k * 128:(k + 1) * 128])
            s.dma(gb[:, P + k:P + k + 1], be2_in[k * 128:(k + 1) * 128])
            s.dma(ob_sb[:, k:k + 1], ob_in[k * 128:(k + 1) * 128])
            s.dma(b2_sb[:, k:k + 1], b2_in[k * 128:(k + 1) * 128])
        for k in range(NF):
            s.dma(b1_sb[:, k:k + 1], b1_in[k * 128:(k + 1) * 128])
        s.dma(kvz[:], kvz_in[0, :, :])
        for r in range(1, 4):
            s.dma(kvp[:], kvz_in[r, :, :])
            s.op('vector', lambda e: e.tensor_tensor(out=kvz[:], in0=kvz[:], in1=kvp[:], op=ALU.add))
        s.op('scalar', lambda e: e.copy(out=kvzb[:], in_=kvz[:]))
        for blk in range(NBLK):
            b0 = blk * 512
            for k in range(P):
                s.dma(x2_t[k][:], x2_in[k * 128:(k + 1) * 128, b0:b0 + 512])
            for k in range(P):
                s.dma(qf_t[k][:], qfT_in[k * 128:(k + 1) * 128, b0:b0 + 512])
            ln_T(s, c, [x2_t[k][:] for k in range(P)], [xi_t[k][:] for k in range(P)])
            for h in range(H):
                hp, sub = h // 2, h % 2
                r0 = sub * 64
                qf_ap = qf_t[h // 2][(h % 2) * 64:(h % 2) * 64 + 64, :]
                mm(s, ps[2][0:64, :], kvzb[r0:r0 + 64, hp * 65:hp * 65 + 64], qf_ap, True, True)
                mm(s, ps[3][0:1, :], kvzb[r0:r0 + 64, hp * 65 + 64:hp * 65 + 65], qf_ap, True, True)
                s.op('vector', lambda e: e.tensor_scalar(out=rec[:], in0=ps[3][0:1, :], scalar1=1e-6, scalar2=None, op0=ALU.max))
                s.op('vector', lambda e: e.reciprocal(rec[:], rec[:]))
                mm(s, ps[4][0:64, :], c['ones_row'][0:1, 0:64], rec[0:1, :], True, True)
                s.op('scalar', lambda e: e.copy(out=num_sb[:], in_=ps[2][0:64, :]))
                s.op('vector', lambda e, h=h: e.tensor_tensor(out=yT[h // 2][(h % 2) * 64:(h % 2) * 64 + 64, :], in0=num_sb[:], in1=ps[4][0:64, :], op=ALU.mult))
            for m in range(P):
                for k in range(P):
                    mm(s, ps[2][:, :], wo[:, k * D + m * 128: k * D + m * 128 + 128], yT[k][:], k == 0, k == P - 1)
                s.op('vector', lambda e, m=m: e.tensor_scalar(out=ev[:], in0=ps[2][:, :], scalar1=ob_sb[:, m:m + 1], scalar2=None, op0=ALU.add))
                s.op('vector', lambda e, m=m: e.tensor_tensor(out=xi_t[m][:], in0=xi_t[m][:], in1=ev[:], op=ALU.add))
            ln_T(s, c, [xi_t[k][:] for k in range(P)], [ln2[k][:] for k in range(P)], gain=gb[:, 0:P], bias=gb[:, P:2 * P])
            for m in range(NF):
                for k in range(P):
                    mm(s, ps[2][:, :], w1[:, k * DFF + m * 128: k * DFF + m * 128 + 128], ln2[k][:], k == 0, k == P - 1)
                s.op('vector', lambda e, m=m: e.tensor_scalar(out=ev[:], in0=ps[2][:, :], scalar1=b1_sb[:, m:m + 1], scalar2=None, op0=ALU.add))
                s.op('scalar', lambda e, m=m: e.activation(out=h_bf[m][:], in_=ev[:], func=AF.Gelu))
            for m in range(P):
                for half in range(2):
                    hk0 = half * (DFF // 2)
                    s.dma(w2m[:, half * (NF // 2) * 128:(half + 1) * (NF // 2) * 128].rearrange("p (k c) -> p k c", c=128),
                          w2_in[hk0:hk0 + DFF // 2, m * 128:(m + 1) * 128].rearrange("(k p) c -> p k c", p=128))
                for k in range(NF):
                    mm(s, ps[2][:, :], w2m[:, k * 128:(k + 1) * 128], h_bf[k][:], k == 0, k == NF - 1)
                s.op('vector', lambda e, m=m: e.tensor_scalar(out=ev[:], in0=ps[2][:, :], scalar1=b2_sb[:, m:m + 1], scalar2=None, op0=ALU.add))
                s.op('vector', lambda e, m=m: e.tensor_tensor(out=ev[:], in0=ev[:], in1=xi_t[m][:], op=ALU.add))
                s.op('vector', lambda e, m=m: e.tensor_tensor(out=ev[:], in0=ev[:], in1=x2_t[m][:], op=ALU.add))
                s.dma(x4_o[m * 128:(m + 1) * 128, b0:b0 + 512], ev[:])
        emit(nc, s)
    return nc


def build_F():
    nc = bass.Bass("TRN2", target_bir_lowering=False)
    s = Sched()
    x4_in = nc.dram_tensor("x4", [D, TPC], F32, kind="ExternalInput").ap()
    wgT_in = nc.dram_tensor("wgT", [D, HID3], BF16, kind="ExternalInput").ap()
    wuT_in = nc.dram_tensor("wuT", [D, HID3], BF16, kind="ExternalInput").ap()
    wdT_in = nc.dram_tensor("wdT", [HID3, D], BF16, kind="ExternalInput").ap()
    id_in = nc.dram_tensor("ident", [128, 128], F32, kind="ExternalInput").ap()
    out_o = nc.dram_tensor("out", [TPC, D], F32, kind="ExternalOutput").ap()
    NK = HID3 // 128
    with ExitStack() as st:
        es = st.enter_context
        c = ln_common(nc, s, es)
        ps = c['ps']
        wg = es(nc.sbuf_tensor("wgF", [128, P * HID3], BF16))
        wu = es(nc.sbuf_tensor("wuF", [128, P * HID3], BF16))
        wdk = es(nc.sbuf_tensor("wdkF", [128, 512], BF16))
        ident = es(nc.sbuf_tensor("identF", [128, 128], F32))
        x_t = [es(nc.sbuf_tensor(f"xtF{k}", [128, 512], F32)) for k in range(P)]
        lnT = [es(nc.sbuf_tensor(f"lnF{k}", [128, 512], BF16)) for k in range(P)]
        h1 = es(nc.sbuf_tensor("h1F", [128, 512], BF16))
        h_bf = [es(nc.sbuf_tensor(f"hbfF{k}", [128, 512], BF16)) for k in range(NK)]
        xn = es(nc.sbuf_tensor("xnF", [128, 512], F32))
        ev = es(nc.sbuf_tensor("evF", [128, 512], F32))
        s.dma(ident[:], id_in[:])
        for k in range(P):
            s.dma(wg[:, k * HID3:(k + 1) * HID3], wgT_in[k * 128:(k + 1) * 128, :])
            s.dma(wu[:, k * HID3:(k + 1) * HID3], wuT_in[k * 128:(k + 1) * 128, :])
        for blk in range(NBLK):
            for k in range(P):
                s.dma(x_t[k][:], x4_in[k * 128:(k + 1) * 128, blk * 512:blk * 512 + 512])
            ln_T(s, c, [x_t[k][:] for k in range(P)], [lnT[k][:] for k in range(P)])
            for m in range(NK):
                for k in range(P):
                    mm(s, ps[2][:, :], wg[:, k * HID3 + m * 128: k * HID3 + m * 128 + 128], lnT[k][:], k == 0, k == P - 1)
                s.op('scalar', lambda e: e.activation(out=h1[:], in_=ps[2][:, :], func=AF.Silu))
                for k in range(P):
                    mm(s, ps[3][:, :], wu[:, k * HID3 + m * 128: k * HID3 + m * 128 + 128], lnT[k][:], k == 0, k == P - 1)
                s.op('vector', lambda e, m=m: e.tensor_tensor(out=h_bf[m][:], in0=h1[:], in1=ps[3][:, :], op=ALU.mult))
            for n2 in range(2):
                for k in range(NK):
                    s.dma(wdk[:], wdT_in[k * 128:(k + 1) * 128, n2 * 512:(n2 + 1) * 512])
                    for tt in range(4):
                        mm(s, ps[2 + tt][:, :], h_bf[k][:, tt * 128:tt * 128 + 128], wdk[:], k == 0, k == NK - 1)
                for tt in range(4):
                    t0 = (blk * 4 + tt) * 128
                    for q in range(4):
                        dchunk = n2 * 4 + q
                        s.op('tensor', lambda e, dchunk=dchunk, tt=tt, q=q: e.transpose(ps[6][:, q * 128:(q + 1) * 128], x_t[dchunk][:, tt * 128:tt * 128 + 128], ident[:]))
                    s.op('scalar', lambda e: e.copy(out=xn[:], in_=ps[6][:, :]))
                    s.op('vector', lambda e, tt=tt: e.tensor_tensor(out=ev[:], in0=xn[:], in1=ps[2 + tt][:, :], op=ALU.add))
                    s.dma(out_o[t0:t0 + 128, n2 * 512:(n2 + 1) * 512], ev[:])
        emit(nc, s)
    return nc


def _bf(a):
    return np.asarray(a).astype(BF)


def _rope_tables():
    inv = 1.0 / (10000.0 ** (np.arange(0, DH, 2, dtype=np.float64) / DH))
    pos = np.arange(T, dtype=np.float64)
    fr = pos[:, None] * inv[None, :]
    cos = np.cos(fr).astype(np.float32).T
    sin = np.sin(fr).astype(np.float32).T
    Cf = np.tile(cos, (4, 1))
    Sf = np.concatenate([-sin, sin, -sin, sin], axis=0)
    sc = np.float32(1.0 / np.sqrt(DH))
    return Cf, Sf, (Cf * sc).astype(np.float32), (Sf * sc).astype(np.float32)


def _masks():
    ones = np.ones((128, 512), np.float32)
    zeros = np.zeros((128, 512), np.float32)
    f = np.arange(512)[None, :]
    p = np.arange(128)[:, None]
    diags = [(r * 128 + p <= f).astype(np.float32) for r in range(4)]
    out = []
    for cidx in range(NCORES):
        tiles = []
        for j in range(4):
            m = 8 * j + cidx
            i = 15 - m // 2
            for n in range(FJ[j]):
                if n < 4 * i:
                    tiles.append(ones)
                elif n < 4 * (i + 1):
                    tiles.append(diags[n - 4 * i])
                else:
                    tiles.append(zeros)
        out.append(_bf(np.concatenate(tiles, axis=1)))
    return out


def _block_of(cidx, j):
    m = 8 * j + cidx
    return (m % 2, 15 - m // 2)


_cache = {}


def _get(name, fn):
    if name not in _cache:
        _cache[name] = fn()
    return _cache[name]


def kernel(**inp):
    cores = list(range(NCORES))
    ident = np.eye(128, dtype=np.float32)
    identb = _bf(ident)

    srcs = {
        "wqkv": inp['L0_attn_Wqkv'], "wout": inp['L0_attn_Wout'],
        "sw1": inp['L0_ffn_sw1'], "sw2": inp['L0_ffn_sw2'], "sw3": inp['L0_ffn_sw3'],
        "ew1": np.asarray(inp['L0_ffn_ew1']).reshape(8192, 2048),
        "ew2": np.asarray(inp['L0_ffn_ew2']).reshape(8192, 2048),
        "ew3": np.asarray(inp['L0_ffn_ew3']).reshape(16384, 1024),
        "rw": inp['L0_ffn_rw'],
        "qkvw": inp['L1_attn_qkv_w'], "outw": inp['L1_attn_out_w'],
        "w1": inp['L1_attn_w1'], "w2": inp['L1_attn_w2'],
        "wgT": np.ascontiguousarray(np.asarray(inp['L1_ffn_w_gate']).T),
        "wuT": np.ascontiguousarray(np.asarray(inp['L1_ffn_w_up']).T),
        "wdT": np.ascontiguousarray(np.asarray(inp['L1_ffn_w_down']).T),
    }
    ncP = _get('P', build_prep)
    in_maps = []
    for ci in cores:
        m = {}
        for (name, rows, cols, sdt) in PREP_JOBS:
            m[name + "_in"] = np.ascontiguousarray(np.asarray(srcs[name]).reshape(-1, cols)[ci * rows:(ci + 1) * rows])
        in_maps.append(m)
    resP = run_bass_kernel_spmd(ncP, in_maps, core_ids=cores)
    W = {}
    for (name, rows, cols, sdt) in PREP_JOBS:
        W[name] = np.concatenate([resP.results[ci][name + "_out"] for ci in cores], axis=0)

    Cf, Sf, Cq, Sq = _get('rope', _rope_tables)
    x_flat = np.asarray(inp['L0_attn_x']).reshape(NT, D)
    ncA = _get('A', build_A)
    in_maps = []
    for ci in cores:
        pos0 = (ci % 4) * TPC
        in_maps.append({
            "x": np.ascontiguousarray(x_flat[ci * TPC:(ci + 1) * TPC]),
            "wqkv": W['wqkv'],
            "cq": np.ascontiguousarray(Cq[:, pos0:pos0 + TPC]),
            "sq": np.ascontiguousarray(Sq[:, pos0:pos0 + TPC]),
            "ck": np.ascontiguousarray(Cf[:, pos0:pos0 + TPC]),
            "sk": np.ascontiguousarray(Sf[:, pos0:pos0 + TPC]),
            "ident": ident, "identb": identb,
        })
    resA = run_bass_kernel_spmd(ncA, in_maps, core_ids=cores)
    xT0 = np.concatenate([resA.results[ci]["xT"] for ci in cores], axis=1)
    qT = np.concatenate([resA.results[ci]["qT"] for ci in cores], axis=1)
    kT = np.concatenate([resA.results[ci]["kT"] for ci in cores], axis=1)
    vfull = [np.concatenate([resA.results[ci]["v"] for ci in range(4 * b, 4 * b + 4)], axis=1) for b in range(2)]

    masks = _get('masks', _masks)
    ncB = _get('B', build_B)
    in_maps = []
    for ci in cores:
        m = {"masks": masks[ci], "wout": W['wout']}
        qcols, x0cols = [], []
        for j in range(4):
            b, i = _block_of(ci, j)
            c0 = b * T + i * 512
            qcols.append(qT[:, c0:c0 + 512])
            x0cols.append(xT0[:, c0:c0 + 512])
            F = FJ[j]
            m[f"kT{j}"] = np.ascontiguousarray(kT[:, b * T:b * T + F * 128])
            vv = vfull[b][:, :F * 128, :].reshape(H, F, 128, DH).transpose(0, 2, 1, 3)
            vv = np.concatenate([vv, np.ones((H, 128, F, 1), BF)], axis=3)
            m[f"v{j}"] = np.ascontiguousarray(vv.reshape(H, 128, F * 65))
        m["qT"] = np.ascontiguousarray(np.concatenate(qcols, axis=1))
        m["x0"] = np.ascontiguousarray(np.concatenate(x0cols, axis=1))
        in_maps.append(m)
    resB = run_bass_kernel_spmd(ncB, in_maps, core_ids=cores)
    xT1 = np.empty((D, NT), np.float32)
    for ci in cores:
        for j in range(4):
            b, i = _block_of(ci, j)
            xT1[:, b * T + i * 512: b * T + (i + 1) * 512] = resB.results[ci]["xT1"][:, j * 512:(j + 1) * 512]

    ncC = _get('C', build_C)
    in_maps = [{
        "x1": np.ascontiguousarray(xT1[:, ci * TPC:(ci + 1) * TPC]),
        "sw1": W['sw1'], "sw2": W['sw2'], "sw3": W['sw3'],
        "ew1": W['ew1'], "ew2": W['ew2'], "ew3": W['ew3'],
        "rw": W['rw'], "rb": np.ascontiguousarray(np.asarray(inp['L0_ffn_rb'])),
        "ident": ident,
    } for ci in cores]
    resC = run_bass_kernel_spmd(ncC, in_maps, core_ids=cores)
    xT2 = np.concatenate([resC.results[ci]["x2"] for ci in cores], axis=1)

    ncD = _get('D', build_D)
    in_maps = [{
        "x2": np.ascontiguousarray(xT2[:, ci * TPC:(ci + 1) * TPC]),
        "qkvw": W['qkvw'], "qkvb": np.ascontiguousarray(np.asarray(inp['L1_attn_qkv_b'])),
        "g1": np.ascontiguousarray(np.asarray(inp['L1_attn_g1'])),
        "be1": np.ascontiguousarray(np.asarray(inp['L1_attn_be1'])),
    } for ci in cores]
    resD = run_bass_kernel_spmd(ncD, in_maps, core_ids=cores)
    qfT = np.concatenate([resD.results[ci]["qfT"] for ci in cores], axis=1)
    kvz = np.stack([resD.results[ci]["kvz"] for ci in cores])

    ncE = _get('E', build_E)
    in_maps = [{
        "x2": np.ascontiguousarray(xT2[:, ci * TPC:(ci + 1) * TPC]),
        "qfT": np.ascontiguousarray(qfT[:, ci * TPC:(ci + 1) * TPC]),
        "kvzp": np.ascontiguousarray(kvz[4 * (ci // 4):4 * (ci // 4) + 4]),
        "outw": W['outw'], "outb": np.ascontiguousarray(np.asarray(inp['L1_attn_out_b'])),
        "w1": W['w1'], "b1": np.ascontiguousarray(np.asarray(inp['L1_attn_b1'])),
        "w2": W['w2'], "b2": np.ascontiguousarray(np.asarray(inp['L1_attn_b2'])),
        "g2": np.ascontiguousarray(np.asarray(inp['L1_attn_g2'])),
        "be2": np.ascontiguousarray(np.asarray(inp['L1_attn_be2'])),
    } for ci in cores]
    resE = run_bass_kernel_spmd(ncE, in_maps, core_ids=cores)
    xT4 = np.concatenate([resE.results[ci]["x4"] for ci in cores], axis=1)

    ncF = _get('F', build_F)
    in_maps = [{
        "x4": np.ascontiguousarray(xT4[:, ci * TPC:(ci + 1) * TPC]),
        "wgT": W['wgT'], "wuT": W['wuT'], "wdT": W['wdT'],
        "ident": ident,
    } for ci in cores]
    resF = run_bass_kernel_spmd(ncF, in_maps, core_ids=cores)
    out = np.concatenate([resF.results[ci]["out"] for ci in cores], axis=0)
    return np.ascontiguousarray(out.reshape(B, T, D).astype(np.float32))


# version 04 (final)
# speedup vs baseline: 1.0176x; 1.0176x over previous
import numpy as np
import ml_dtypes
from contextlib import ExitStack
import concourse.bass as bass
import concourse.mybir as mybir
from concourse.bass_utils import run_bass_kernel_spmd

F32 = mybir.dt.float32
BF16 = mybir.dt.bfloat16
I32 = mybir.dt.int32
AF = mybir.ActivationFunctionType
ALU = mybir.AluOpType
AX = mybir.AxisListType
BF = ml_dtypes.bfloat16

NCORES = 8
B, T, D = 2, 8192, 1024
NT = B * T
TPC = NT // NCORES
NBLK = TPC // 512
P = D // 128
H, DH = 16, 64
EPS = 1e-5
FJ = [64, 48, 32, 16]
NEXP = 8
HID = 2048
DFF = 4096
HID3 = 3072


class Sched:
    def __init__(self):
        self.steps = []
        self.t = 0
        self.pf = 0
        self.labels = {}

    def label(self, name):
        self.labels[name] = self.t

    def add(self, eng, fn, inc=1, kind='c', co=False, wait=None, pfw=None, sem='tk'):
        self.steps.append([eng, fn, self.t, inc, kind, co, wait, pfw, sem])
        if sem == 'tk':
            self.t += inc
        else:
            self.pf += inc

    def dma(self, out, in_, eng='sync', wait=None, pfw=None, **kw):
        self.add(eng, (lambda e, o=out, i=in_, kw=kw: e.dma_start(out=o, in_=i, **kw)), 16, kind='dma', wait=wait, pfw=pfw)

    def dma_pf(self, out, in_, eng='sync', wait=None, **kw):
        self.add(eng, (lambda e, o=out, i=in_, kw=kw: e.dma_start(out=o, in_=i, **kw)), 16, kind='dma', wait=wait, sem='pf')

    def op(self, eng, fn, co=False, wait=None, pfw=None):
        self.add(eng, fn, 1, kind='c', co=co, wait=wait, pfw=pfw)


def emit(nc, s):
    skip = [False] * len(s.steps)
    for i in range(1, len(s.steps)):
        pe, _, _, _, pk, _, pw, _, psem = s.steps[i - 1]
        ce, _, _, _, ck, cco, cw, cpf, csem = s.steps[i]
        if cw is None and cpf is None and pe == ce and (ce == 'tensor' or (pk == 'dma' and ck == 'dma') or cco):
            skip[i] = True
    with nc.semaphore("tick") as TK, nc.semaphore("pfs") as PF:
        with nc.Block() as block:
            def run(eng_obj, name):
                for i, (engname, fn, tb, inc, kind, co, wait, pfw, sem) in enumerate(s.steps):
                    if engname == name:
                        wv = s.labels[wait] if wait is not None else tb
                        if wv > 0 and not skip[i]:
                            eng_obj.wait_ge(TK, wv)
                        if pfw:
                            eng_obj.wait_ge(PF, pfw)
                        inst = fn(eng_obj)
                        inst.then_inc(TK if sem == 'tk' else PF, inc)
                if name == 'vector':
                    eng_obj.wait_ge(TK, s.t)
                    if s.pf > 0:
                        eng_obj.wait_ge(PF, s.pf)

            @block.sync
            def _(e):
                run(e, 'sync')

            @block.scalar
            def _(e):
                run(e, 'scalar')

            @block.vector
            def _(e):
                run(e, 'vector')

            @block.tensor
            def _(e):
                run(e, 'tensor')

            @block.gpsimd
            def _(e):
                run(e, 'gpsimd')


def mm(s, out, lhsT, rhs, start, stop):
    s.op('tensor', lambda e, o=out, l=lhsT, r=rhs, st=start, sp=stop: e.matmul(o, l, r, start=st, stop=sp))


def mm2(s, out, lhsT, rhs, start, stop, wait=None):
    s.op('tensor', (lambda e, o=out, l=lhsT, r=rhs, st=start, sp=stop: e.matmul(o, l, r, start=st, stop=sp)), wait=wait)


def ln_common(nc, s, es, W=512):
    c = {}
    c['ps'] = [es(nc.psum_tensor(f"psp{i}", [128, 512], F32)) for i in range(7)]
    c['psbf'] = es(nc.psum_tensor("pspbf", [128, 256], BF16))
    c['ones_col'] = es(nc.sbuf_tensor("ones_col", [128, 1], F32))
    c['ones_col_bf'] = es(nc.sbuf_tensor("ones_col_bf", [128, 1], BF16))
    c['ones_row'] = es(nc.sbuf_tensor("ones_row", [1, 128], F32))
    c['mu'] = es(nc.sbuf_tensor("mu", [1, W], F32))
    c['rs'] = es(nc.sbuf_tensor("rs", [1, W], F32))
    c['mub'] = es(nc.sbuf_tensor("mub", [128, W], F32))
    c['rstdb'] = es(nc.sbuf_tensor("rstdb", [128, W], F32))
    c['cen'] = [es(nc.sbuf_tensor(f"cenc{k}", [128, W], BF16)) for k in range(P)]
    c['sqc'] = es(nc.sbuf_tensor("sqc", [128, W], BF16))
    c['tmp'] = es(nc.sbuf_tensor("tmpf", [128, W], F32))
    c['epsb'] = es(nc.sbuf_tensor("epsb", [128, 1], F32))
    s.op('gpsimd', lambda e: e.memset(c['epsb'][:], EPS))
    s.op('gpsimd', lambda e: e.memset(c['ones_col'][:], 1.0))
    s.op('gpsimd', lambda e: e.memset(c['ones_col_bf'][:], 1.0))
    s.op('gpsimd', lambda e: e.memset(c['ones_row'][:], 1.0))
    return c


def ln_T(s, c, x_tiles, out_tiles, gain=None, bias=None):
    n = len(x_tiles)
    ps = c['ps']
    for k in range(n):
        mm(s, ps[0][0:1, :], c['ones_col'][:], x_tiles[k], k == 0, k == n - 1)
    s.op('scalar', lambda e: e.mul(out=c['mu'][0:1, :], in_=ps[0][0:1, :], mul=1.0 / D))
    mm(s, ps[1][:, :], c['ones_row'][0:1, :], c['mu'][0:1, :], True, True)
    s.op('scalar', lambda e: e.copy(out=c['mub'][:], in_=ps[1][:, :]))
    for k in range(n):
        s.op('vector', lambda e, k=k: e.tensor_tensor(out=c['cen'][k][:], in0=x_tiles[k], in1=c['mub'][:], op=ALU.subtract))
    for k in range(n):
        s.op('scalar', lambda e, k=k: e.activation(out=c['sqc'][:], in_=c['cen'][k][:], func=AF.Square))
        mm(s, ps[0][0:1, :], c['ones_col_bf'][:], c['sqc'][:], k == 0, k == n - 1)
    s.op('scalar', lambda e: e.activation(out=c['mu'][0:1, :], in_=ps[0][0:1, :], func=AF.Ln, bias=c['epsb'][0:1, :], scale=1.0 / D))
    s.op('scalar', lambda e: e.activation(out=c['rs'][0:1, :], in_=c['mu'][0:1, :], func=AF.Exp, scale=-0.5))
    mm(s, ps[1][:, :], c['ones_row'][0:1, :], c['rs'][0:1, :], True, True)
    s.op('scalar', lambda e: e.copy(out=c['rstdb'][:], in_=ps[1][:, :]))
    for k in range(n):
        if gain is None:
            s.op('vector', lambda e, k=k: e.tensor_tensor(out=out_tiles[k], in0=c['cen'][k][:], in1=c['rstdb'][:], op=ALU.mult))
        else:
            s.op('vector', lambda e, k=k: e.tensor_tensor(out=c['tmp'][:], in0=c['cen'][k][:], in1=c['rstdb'][:], op=ALU.mult))
            s.op('vector', lambda e, k=k, g=gain, b=bias: e.tensor_scalar(out=out_tiles[k], in0=c['tmp'][:], scalar1=g[:, k:k + 1], scalar2=b[:, k:k + 1], op0=ALU.mult, op1=ALU.add))


PREP_JOBS = [
    ("wqkv", 128, 3072, 'f32'), ("wout", 128, 1024, 'f32'),
    ("sw1", 128, 2048, 'f32'), ("sw2", 128, 2048, 'f32'), ("sw3", 256, 1024, 'f32'),
    ("ew1", 1024, 2048, 'f32'), ("ew2", 1024, 2048, 'f32'), ("ew3", 2048, 1024, 'f32'),
    ("rw", 128, 8, 'f32'),
    ("qkvw", 128, 3072, 'f32'), ("outw", 128, 1024, 'f32'),
    ("w1", 128, 4096, 'f32'), ("w2", 512, 1024, 'f32'),
    ("wgT", 128, 3072, 'i32'), ("wuT", 128, 3072, 'i32'), ("wdT", 384, 1024, 'i32'),
]


def build_prep():
    nc = bass.Bass("TRN2", target_bir_lowering=False)
    s = Sched()
    with ExitStack() as st:
        es = st.enter_context
        buf_f = es(nc.sbuf_tensor("buf_f", [128, 4096], F32))
        buf_i = es(nc.sbuf_tensor("buf_i", [128, 4096], I32))
        buf_b = es(nc.sbuf_tensor("buf_b", [128, 4096], BF16))
        for (name, rows, cols, sdt) in PREP_JOBS:
            src = nc.dram_tensor(name + "_in", [rows, cols], F32 if sdt == 'f32' else I32, kind="ExternalInput").ap()
            dst = nc.dram_tensor(name + "_out", [rows, cols], BF16, kind="ExternalOutput").ap()
            for r0 in range(0, rows, 128):
                if sdt == 'f32':
                    s.dma(buf_f[:, 0:cols], src[r0:r0 + 128, :])
                    s.op('scalar', lambda e, cols=cols: e.copy(out=buf_b[:, 0:cols], in_=buf_f[:, 0:cols]))
                else:
                    s.dma(buf_i[:, 0:cols], src[r0:r0 + 128, :])
                    s.op('vector', lambda e, cols=cols: e.tensor_copy(out=buf_f[:, 0:cols], in_=buf_i[:, 0:cols]))
                    s.op('scalar', lambda e, cols=cols: e.mul(out=buf_b[:, 0:cols], in_=buf_f[:, 0:cols], mul=1.0 / 127.0))
                s.dma(dst[r0:r0 + 128, :], buf_b[:, 0:cols])
        emit(nc, s)
    return nc


def build_A():
    nc = bass.Bass("TRN2", target_bir_lowering=False)
    s = Sched()
    x_in = nc.dram_tensor("x", [TPC, D], F32, kind="ExternalInput").ap()
    wq_in = nc.dram_tensor("wqkv", [D, 3 * D], BF16, kind="ExternalInput").ap()
    CS_in = {n: nc.dram_tensor(n, [128, TPC], F32, kind="ExternalInput").ap() for n in ['cq', 'sq', 'ck', 'sk']}
    id_in = nc.dram_tensor("ident", [128, 128], F32, kind="ExternalInput").ap()
    idb_in = nc.dram_tensor("identb", [128, 128], BF16, kind="ExternalInput").ap()
    xT_o = nc.dram_tensor("xT", [D, TPC], F32, kind="ExternalOutput").ap()
    qT_o = nc.dram_tensor("qT", [D, TPC], BF16, kind="ExternalOutput").ap()
    kT_o = nc.dram_tensor("kT", [D, TPC], BF16, kind="ExternalOutput").ap()
    v_o = nc.dram_tensor("v", [H, TPC, DH], BF16, kind="ExternalOutput").ap()
    with ExitStack() as st:
        es = st.enter_context
        ps = [es(nc.psum_tensor(f"psA{i}", [128, 512], F32)) for i in range(4)]
        psbf = es(nc.psum_tensor("psAbf", [128, 256], BF16))
        wq = es(nc.sbuf_tensor("wqA", [128, P * 3 * D], BF16))
        CS = {n: es(nc.sbuf_tensor(n + "_sb", [128, TPC], F32)) for n in ['cq', 'sq', 'ck', 'sk']}
        ident = es(nc.sbuf_tensor("identA", [128, 128], F32))
        identb = es(nc.sbuf_tensor("identbA", [128, 128], BF16))
        x_sb = es(nc.sbuf_tensor("x_sbA", [128, D], F32))
        stat = es(nc.sbuf_tensor("statA", [128, 1], F32))
        stat2 = es(nc.sbuf_tensor("stat2A", [128, 1], F32))
        cen = es(nc.sbuf_tensor("cenA", [128, D], F32))
        sqA = es(nc.sbuf_tensor("sqA", [128, D], F32))
        ln_bf = es(nc.sbuf_tensor("ln_bfA", [128, D], BF16))
        lnT = [es(nc.sbuf_tensor(f"lnTA{k}", [128, 512], BF16)) for k in range(P)]
        ev = es(nc.sbuf_tensor("evA", [128, 512], F32))
        ev2 = es(nc.sbuf_tensor("ev2A", [128, 512], F32))
        evs = es(nc.sbuf_tensor("evsA", [128, 512], F32))
        out_bf = es(nc.sbuf_tensor("obfA", [128, 512], BF16))
        epsA = es(nc.sbuf_tensor("epsA", [128, 1], F32))
        s.op('gpsimd', lambda e: e.memset(epsA[:], EPS))
        s.dma(ident[:], id_in[:])
        s.dma(identb[:], idb_in[:])
        for k in range(P):
            s.dma(wq[:, k * 3 * D:(k + 1) * 3 * D], wq_in[k * 128:(k + 1) * 128, :])
        for n in CS:
            s.dma(CS[n][:], CS_in[n][:])

        def wslice(k, col, width=128):
            return wq[:, k * 3 * D + col: k * 3 * D + col + width]

        for blk in range(NBLK):
            c0 = blk * 512
            for tt in range(4):
                t0 = (blk * 4 + tt) * 128
                s.dma(x_sb[:], x_in[t0:t0 + 128, :])
                s.op('vector', lambda e: e.reduce_sum(stat[:], x_sb[:], axis=AX.X))
                s.op('scalar', lambda e: e.mul(out=stat[:], in_=stat[:], mul=-1.0 / D))
                s.op('vector', lambda e: e.tensor_scalar(out=cen[:], in0=x_sb[:], scalar1=stat[:], scalar2=None, op0=ALU.add))
                s.op('scalar', lambda e: e.activation(out=sqA[:], in_=cen[:], func=AF.Square))
                s.op('vector', lambda e: e.reduce_sum(stat2[:], sqA[:], axis=AX.X))
                s.op('scalar', lambda e: e.activation(out=stat2[:], in_=stat2[:], func=AF.Ln, bias=epsA[:], scale=1.0 / D))
                s.op('scalar', lambda e: e.activation(out=stat2[:], in_=stat2[:], func=AF.Exp, scale=-0.5))
                s.op('vector', lambda e: e.tensor_scalar(out=ln_bf[:], in0=cen[:], scalar1=stat2[:], scalar2=None, op0=ALU.mult))
                for k in range(P):
                    s.op('tensor', lambda e, k=k: e.transpose(ps[2][:, 0:128], x_sb[:, k * 128:(k + 1) * 128], ident[:]))
                    s.op('scalar', lambda e: e.copy(out=ev[:, 0:128], in_=ps[2][:, 0:128]))
                    s.dma(xT_o[k * 128:(k + 1) * 128, t0:t0 + 128], ev[:, 0:128])
                    s.op('tensor', lambda e, k=k: e.transpose(psbf[:, 0:128], ln_bf[:, k * 128:(k + 1) * 128], identb[:]))
                    s.op('scalar', lambda e, k=k, tt=tt: e.copy(out=lnT[k][:, tt * 128:tt * 128 + 128], in_=psbf[:, 0:128]))
            for (col0, cs_c, cs_s, out_t) in [(0, 'cq', 'sq', qT_o), (D, 'ck', 'sk', kT_o)]:
                for m in range(P):
                    for k in range(P):
                        mm(s, ps[3][:, :], wslice(k, col0 + m * 128), lnT[k][:], k == 0, k == P - 1)
                    s.op('scalar', lambda e: e.copy(out=ev[:], in_=ps[3][:, :]))
                    for g in range(4):
                        a0, src0 = g * 32, (g ^ 1) * 32
                        s.dma(ev2[a0:a0 + 32, :], ev[src0:src0 + 32, :], eng='gpsimd')
                    s.op('vector', lambda e, c0=c0, cs=cs_c: e.tensor_tensor(out=evs[:], in0=ev[:], in1=CS[cs][:, c0:c0 + 512], op=ALU.mult))
                    s.op('vector', lambda e, c0=c0, cs=cs_s: e.tensor_tensor(out=ev2[:], in0=ev2[:], in1=CS[cs][:, c0:c0 + 512], op=ALU.mult))
                    s.op('vector', lambda e: e.tensor_tensor(out=out_bf[:], in0=evs[:], in1=ev2[:], op=ALU.add))
                    s.dma(out_t[m * 128:(m + 1) * 128, c0:c0 + 512], out_bf[:])
            for tt in range(4):
                t0 = (blk * 4 + tt) * 128
                for n2 in range(2):
                    for k in range(P):
                        mm(s, ps[1][:, :], lnT[k][:, tt * 128:tt * 128 + 128], wslice(k, 2 * D + n2 * 512, 512), k == 0, k == P - 1)
                    s.op('scalar', lambda e: e.copy(out=out_bf[:], in_=ps[1][:, :]))
                    for hh in range(8):
                        h = n2 * 8 + hh
                        s.dma(v_o[h, t0:t0 + 128, :], out_bf[:, hh * 64:hh * 64 + 64])
        emit(nc, s)
    return nc


def build_B():
    nc = bass.Bass("TRN2", target_bir_lowering=False)
    s = Sched()
    qT_in = nc.dram_tensor("qT", [D, 4 * 512], BF16, kind="ExternalInput").ap()
    kT_in = [nc.dram_tensor(f"kT{j}", [D, FJ[j] * 128], BF16, kind="ExternalInput").ap() for j in range(4)]
    v_in = [nc.dram_tensor(f"v{j}", [H, 128, FJ[j] * 65], BF16, kind="ExternalInput").ap() for j in range(4)]
    mask_in = nc.dram_tensor("masks", [128, sum(FJ) * 512], BF16, kind="ExternalInput").ap()
    x0_in = nc.dram_tensor("x0", [D, 4 * 512], F32, kind="ExternalInput").ap()
    wo_in = nc.dram_tensor("wout", [D, D], BF16, kind="ExternalInput").ap()
    xT1_o = nc.dram_tensor("xT1", [D, 4 * 512], F32, kind="ExternalOutput").ap()
    with ExitStack() as st:
        es = st.enter_context
        ps = [es(nc.psum_tensor(f"psB{i}", [128, 512], F32)) for i in range(7)]
        onesr = es(nc.sbuf_tensor("onesrB", [1, 128], F32))
        wo = es(nc.sbuf_tensor("woB", [128, P * D], BF16))
        mask_sb = es(nc.sbuf_tensor("maskB", [128, 64 * 512], BF16))
        k_sb2 = [es(nc.sbuf_tensor(f"k_sbB{i}", [64, 8192], BF16)) for i in range(2)]
        q_sb2 = [es(nc.sbuf_tensor(f"q_sbB{i}", [64, 512], BF16)) for i in range(2)]
        v_sb2 = [es(nc.sbuf_tensor(f"v_sbB{i}", [128, 64 * 65], BF16)) for i in range(2)]
        p2_sb4 = [es(nc.sbuf_tensor(f"p2_sbB{i}", [128, 512], BF16)) for i in range(4)]
        rec = es(nc.sbuf_tensor("recB", [1, 512], F32))
        num_sb = es(nc.sbuf_tensor("numB", [64, 512], F32))
        yT = [es(nc.sbuf_tensor(f"yTB{k}", [128, 512], BF16)) for k in range(P)]
        x0_sb = es(nc.sbuf_tensor("x0B", [128, 512], F32))
        out_sb = es(nc.sbuf_tensor("outB", [128, 512], F32))
        s.op('gpsimd', lambda e: e.memset(onesr[:], 1.0))
        for k in range(P):
            s.dma(wo[:, k * D:(k + 1) * D], wo_in[k * 128:(k + 1) * 128, :])
        moff = 0
        for j in range(4):
            F = FJ[j]
            w = F * 512 // 2
            for half in range(2):
                s.dma(mask_sb[:, half * w:(half + 1) * w], mask_in[:, moff + half * w: moff + (half + 1) * w])
            moff += F * 512
            s.dma(k_sb2[0][:, 0:F * 128], kT_in[j][0:64, :])
            s.dma(q_sb2[0][:], qT_in[0:64, j * 512:(j + 1) * 512])
            s.dma(v_sb2[0][:, 0:F * 65], v_in[j][0, :, :])
            for h in range(H):
                bi = h % 2
                k_sb, q_sb, v_sb = k_sb2[bi], q_sb2[bi], v_sb2[bi]
                need_pf = s.pf if h >= 1 else 0
                if h + 1 < H:
                    wl = f"B{j}h{h - 1}" if h >= 1 else None
                    s.dma_pf(k_sb2[1 - bi][:, 0:F * 128], kT_in[j][(h + 1) * 64:(h + 1) * 64 + 64, :], wait=wl)
                    s.dma_pf(q_sb2[1 - bi][:], qT_in[(h + 1) * 64:(h + 1) * 64 + 64, j * 512:(j + 1) * 512], wait=wl)
                    s.dma_pf(v_sb2[1 - bi][:, 0:F * 65], v_in[j][h + 1, :, :], wait=wl)
                first_mm = True
                for n0 in range(0, F, 4):
                    for u in range(4):
                        n = n0 + u
                        if first_mm and h >= 1:
                            s.add('tensor', (lambda e, o=ps[2 + u][:, :], l=k_sb[0:64, n * 128:(n + 1) * 128], r=q_sb[:]: e.matmul(o, l, r, start=True, stop=True)), 1, pfw=need_pf)
                        else:
                            mm(s, ps[2 + u][:, :], k_sb[0:64, n * 128:(n + 1) * 128], q_sb[:], True, True)
                        first_mm = False
                    for u in range(4):
                        n = n0 + u
                        s.op('scalar', lambda e, u=u, n=n: e.activation(out=p2_sb4[u][:], in_=ps[2 + u][:, :], func=AF.Exp), co=(u > 0))
                    for u in range(4):
                        n = n0 + u
                        s.op('vector', lambda e, u=u, n=n: e.tensor_tensor(out=p2_sb4[u][:], in0=p2_sb4[u][:], in1=mask_sb[:, n * 512:(n + 1) * 512], op=ALU.mult), co=(u > 0))
                    for u in range(4):
                        n = n0 + u
                        mm(s, ps[1][0:65, :], v_sb[:, n * 65:n * 65 + 65], p2_sb4[u][:], n == 0, n == F - 1)
                s.op('vector', lambda e: e.reciprocal(rec[:], ps[1][64:65, :]))
                mm(s, ps[2][0:64, :], onesr[0:1, 0:64], rec[0:1, :], True, True)
                s.op('scalar', lambda e: e.copy(out=num_sb[:], in_=ps[1][0:64, :]))
                s.op('vector', lambda e, h=h: e.tensor_tensor(out=yT[h // 2][(h % 2) * 64:(h % 2) * 64 + 64, :], in0=num_sb[:], in1=ps[2][0:64, :], op=ALU.mult))
                s.label(f"B{j}h{h}")
            for m in range(P):
                for k in range(P):
                    mm(s, ps[3][:, :], wo[:, k * D + m * 128: k * D + m * 128 + 128], yT[k][:], k == 0, k == P - 1)
                s.dma(x0_sb[:], x0_in[m * 128:(m + 1) * 128, j * 512:(j + 1) * 512])
                s.op('vector', lambda e: e.tensor_tensor(out=out_sb[:], in0=x0_sb[:], in1=ps[3][:, :], op=ALU.add))
                s.dma(xT1_o[m * 128:(m + 1) * 128, j * 512:(j + 1) * 512], out_sb[:])
        emit(nc, s)
    return nc


def build_C():
    nc = bass.Bass("TRN2", target_bir_lowering=False)
    s = Sched()
    x_in = nc.dram_tensor("x1", [D, TPC], F32, kind="ExternalInput").ap()
    sw1_in = nc.dram_tensor("sw1", [D, HID], BF16, kind="ExternalInput").ap()
    sw2_in = nc.dram_tensor("sw2", [D, HID], BF16, kind="ExternalInput").ap()
    sw3_in = nc.dram_tensor("sw3", [HID, D], BF16, kind="ExternalInput").ap()
    ew1_in = nc.dram_tensor("ew1", [NEXP * D, HID], BF16, kind="ExternalInput").ap()
    ew2_in = nc.dram_tensor("ew2", [NEXP * D, HID], BF16, kind="ExternalInput").ap()
    ew3_in = nc.dram_tensor("ew3", [NEXP * HID, D], BF16, kind="ExternalInput").ap()
    rw_in = nc.dram_tensor("rw", [D, NEXP], BF16, kind="ExternalInput").ap()
    rb_in = nc.dram_tensor("rb", [NEXP], F32, kind="ExternalInput").ap()
    id_in = nc.dram_tensor("ident", [128, 128], F32, kind="ExternalInput").ap()
    x2_o = nc.dram_tensor("x2", [D, TPC], F32, kind="ExternalOutput").ap()
    gates_d = nc.dram_tensor("gates_scr", [NBLK * NEXP, 512], F32, kind="Internal").ap()
    NH = HID // 128
    with ExitStack() as st:
        es = st.enter_context
        c = ln_common(nc, s, es)
        ps = c['ps']
        wbuf = es(nc.sbuf_tensor("wbuf", [128, 49152], BF16))
        ln_bf = [[es(nc.sbuf_tensor(f"lnbf{b}_{k}", [128, 512], BF16)) for k in range(P)] for b in range(NBLK)]
        gt_sb = es(nc.sbuf_tensor("gt_sb", [8, 512], F32))
        grow = es(nc.sbuf_tensor("grow", [1, 512], F32))
        ident = es(nc.sbuf_tensor("identC", [128, 128], F32))
        rb_sb = es(nc.sbuf_tensor("rb_sb", [8, 1], F32))
        rw_sb = es(nc.sbuf_tensor("rw_sb", [128, P * 8], BF16))
        x_t = [es(nc.sbuf_tensor(f"xtC{k}", [128, 512], F32)) for k in range(P)]
        h1 = es(nc.sbuf_tensor("h1C", [128, 512], BF16))
        h1b = es(nc.sbuf_tensor("h1Cb", [128, 512], BF16))
        ev2C = es(nc.sbuf_tensor("ev2C", [128, 512], F32))
        h_bf = [es(nc.sbuf_tensor(f"hbfC{k}", [128, 512], BF16)) for k in range(NH)]
        pm = es(nc.sbuf_tensor("pmC", [128, 8], F32))
        pmax = es(nc.sbuf_tensor("pmaxC", [128, 1], F32))
        m2 = es(nc.sbuf_tensor("m2C", [128, 1], F32))
        pm2 = es(nc.sbuf_tensor("pm2C", [128, 8], F32))
        gsc = es(nc.sbuf_tensor("gscC", [128, 512], BF16))
        ev = es(nc.sbuf_tensor("evC", [128, 512], F32))
        s.dma(ident[:], id_in[:])
        s.dma(rb_sb[:], rb_in[:])
        for k in range(P):
            s.dma(wbuf[:, k * HID:(k + 1) * HID], sw1_in[k * 128:(k + 1) * 128, :])
            s.dma(wbuf[:, 16384 + k * HID:16384 + (k + 1) * HID], sw2_in[k * 128:(k + 1) * 128, :])
            s.dma(rw_sb[:, k * 8:(k + 1) * 8], rw_in[k * 128:(k + 1) * 128, :])
        for k in range(NH):
            s.dma(wbuf[:, 32768 + k * D:32768 + (k + 1) * D], sw3_in[k * 128:(k + 1) * 128, :])
        for blk in range(NBLK):
            b0 = blk * 512
            for k in range(P):
                s.dma(x_t[k][:], x_in[k * 128:(k + 1) * 128, b0:b0 + 512])
            ln_T(s, c, [x_t[k][:] for k in range(P)], [ln_bf[blk][k][:] for k in range(P)])
            for k in range(P):
                mm(s, ps[2][0:8, :], rw_sb[:, k * 8:(k + 1) * 8], ln_bf[blk][k][:], k == 0, k == P - 1)
            s.op('vector', lambda e: e.tensor_scalar(out=ev[0:8, :], in0=ps[2][0:8, :], scalar1=rb_sb[:], scalar2=None, op0=ALU.add))
            for tt in range(4):
                s.op('tensor', lambda e, tt=tt: e.transpose(ps[3][:, 0:8], ev[0:8, tt * 128:tt * 128 + 128], ident[0:8, 0:8]))
                s.op('scalar', lambda e: e.copy(out=pm[:], in_=ps[3][:, 0:8]))
                s.op('vector', lambda e: e.reduce_max(pmax[:], pm[:], axis=AX.X, negate=True))
                s.op('scalar', lambda e: e.activation(out=pm[:], in_=pm[:], func=AF.Exp, bias=pmax[:], scale=1.0))
                s.op('vector', lambda e: e.reduce_max(pmax[:], pm[:], axis=AX.X))
                s.op('vector', lambda e: e.tensor_scalar(out=pm2[:], in0=pm[:], scalar1=pmax[:], scalar2=-1e30, op0=ALU.is_ge, op1=ALU.mult))
                s.op('vector', lambda e: e.tensor_tensor(out=pm2[:], in0=pm[:], in1=pm2[:], op=ALU.add))
                s.op('vector', lambda e: e.reduce_max(m2[:], pm2[:], axis=AX.X))
                s.op('vector', lambda e: e.tensor_tensor(out=pmax[:], in0=pmax[:], in1=m2[:], op=ALU.add))
                s.op('vector', lambda e: e.reciprocal(pmax[:], pmax[:]))
                s.op('vector', lambda e: e.tensor_scalar(out=pm2[:], in0=pm[:], scalar1=m2[:], scalar2=None, op0=ALU.is_ge))
                s.op('vector', lambda e: e.tensor_tensor(out=pm2[:], in0=pm[:], in1=pm2[:], op=ALU.mult))
                s.op('vector', lambda e: e.tensor_scalar(out=pm2[:], in0=pm2[:], scalar1=pmax[:], scalar2=None, op0=ALU.mult))
                s.op('tensor', lambda e: e.transpose(ps[3][0:8, 0:128], pm2[:], ident[:]))
                s.op('scalar', lambda e, tt=tt: e.copy(out=gt_sb[:, tt * 128:tt * 128 + 128], in_=ps[3][0:8, 0:128]))
            s.dma(gates_d[blk * 8:(blk + 1) * 8, :], gt_sb[:], eng='gpsimd')
            for m in range(NH):
                for k in range(P):
                    mm(s, ps[2][:, :], wbuf[:, k * HID + m * 128: k * HID + m * 128 + 128], ln_bf[blk][k][:], k == 0, k == P - 1)
                s.op('scalar', lambda e: e.activation(out=h1[:], in_=ps[2][:, :], func=AF.Silu))
                for k in range(P):
                    mm(s, ps[3][:, :], wbuf[:, 16384 + k * HID + m * 128: 16384 + k * HID + m * 128 + 128], ln_bf[blk][k][:], k == 0, k == P - 1)
                s.op('vector', lambda e, m=m: e.tensor_tensor(out=h_bf[m][:], in0=h1[:], in1=ps[3][:, :], op=ALU.mult))
            for m in range(P):
                for k in range(NH):
                    mm(s, ps[2][:, :], wbuf[:, 32768 + k * D + m * 128: 32768 + k * D + m * 128 + 128], h_bf[k][:], k == 0, k == NH - 1)
                s.op('vector', lambda e, m=m: e.tensor_tensor(out=x_t[m][:], in0=x_t[m][:], in1=ps[2][:, :], op=ALU.add))
                s.dma(x2_o[m * 128:(m + 1) * 128, b0:b0 + 512], x_t[m][:])
        for ex in range(NEXP):
            for k in range(P):
                s.dma(wbuf[:, k * HID:(k + 1) * HID], ew1_in[ex * D + k * 128:ex * D + (k + 1) * 128, :])
                s.dma(wbuf[:, 16384 + k * HID:16384 + (k + 1) * HID], ew2_in[ex * D + k * 128:ex * D + (k + 1) * 128, :])
            for k in range(NH):
                s.dma(wbuf[:, 32768 + k * D:32768 + (k + 1) * D], ew3_in[ex * HID + k * 128:ex * HID + (k + 1) * 128, :])
            for blk in range(NBLK):
                b0 = blk * 512
                for m in range(NH):
                    for k in range(P):
                        mm(s, ps[2][:, :], wbuf[:, k * HID + m * 128: k * HID + m * 128 + 128], ln_bf[blk][k][:], k == 0, k == P - 1)
                    for k in range(P):
                        mm(s, ps[3][:, :], wbuf[:, 16384 + k * HID + m * 128: 16384 + k * HID + m * 128 + 128], ln_bf[blk][k][:], k == 0, k == P - 1)
                    s.op('scalar', lambda e: e.activation(out=h1[:], in_=ps[2][:, :], func=AF.Silu))
                    s.op('vector', lambda e, m=m: e.tensor_tensor(out=h_bf[m][:], in0=h1[:], in1=ps[3][:, :], op=ALU.mult))
                s.dma(grow[:], gates_d[blk * 8 + ex:blk * 8 + ex + 1, :])
                mm(s, ps[4][:, :], c['ones_row'][0:1, :], grow[0:1, :], True, True)
                s.op('scalar', lambda e: e.copy(out=gsc[:], in_=ps[4][:, :]))
                for m in range(NH):
                    s.op('vector', (lambda e, m=m: e.tensor_tensor(out=h_bf[m][:], in0=h_bf[m][:], in1=gsc[:], op=ALU.mult)), co=(m > 0))
                for m in range(P):
                    for k in range(NH):
                        mm(s, ps[2][:, :], wbuf[:, 32768 + k * D + m * 128: 32768 + k * D + m * 128 + 128], h_bf[k][:], k == 0, k == NH - 1)
                    s.op('scalar', lambda e: e.copy(out=ev[:], in_=ps[2][:, :]))
                    s.dma(x2_o[m * 128:(m + 1) * 128, b0:b0 + 512], ev[:], eng='gpsimd', accum_op=ALU.add)
        emit(nc, s)
    return nc


def build_D():
    nc = bass.Bass("TRN2", target_bir_lowering=False)
    s = Sched()
    x_in = nc.dram_tensor("x2", [D, TPC], F32, kind="ExternalInput").ap()
    wq_in = nc.dram_tensor("qkvw", [D, 3 * D], BF16, kind="ExternalInput").ap()
    qb_in = nc.dram_tensor("qkvb", [3 * D], F32, kind="ExternalInput").ap()
    g1_in = nc.dram_tensor("g1", [D], F32, kind="ExternalInput").ap()
    be1_in = nc.dram_tensor("be1", [D], F32, kind="ExternalInput").ap()
    qfT_o = nc.dram_tensor("qfT", [D, TPC], BF16, kind="ExternalOutput").ap()
    kvz_o = nc.dram_tensor("kvz", [128, 520], F32, kind="ExternalOutput").ap()
    with ExitStack() as st:
        es = st.enter_context
        c = ln_common(nc, s, es)
        ps = c['ps']
        wq = es(nc.sbuf_tensor("wqD", [128, P * 3 * D], BF16))
        x_t = [es(nc.sbuf_tensor(f"xtD{k}", [128, 512], F32)) for k in range(P)]
        src_t = [es(nc.sbuf_tensor(f"srcD{k}", [128, 512], F32)) for k in range(P)]
        ln1 = [es(nc.sbuf_tensor(f"ln1D{k}", [128, 512], BF16)) for k in range(P)]
        gb = es(nc.sbuf_tensor("gbD", [128, 2 * P], F32))
        qb_q = es(nc.sbuf_tensor("qbqD", [128, P], F32))
        qrow = es(nc.sbuf_tensor("qrowD", [1, 3 * D], F32))
        bias_b = es(nc.sbuf_tensor("biasbD", [128, 2 * D], F32))
        kf_sb = es(nc.sbuf_tensor("kfD", [128, 16 * D], BF16))
        v_sb = es(nc.sbuf_tensor("vD", [128, 16 * D], BF16))
        onesb = es(nc.sbuf_tensor("onesbD", [128, 1], BF16))
        ev = es(nc.sbuf_tensor("evD", [128, 512], F32))
        ev2 = es(nc.sbuf_tensor("ev2D", [128, 512], F32))
        ev3 = es(nc.sbuf_tensor("ev3D", [128, 512], F32))
        out_bf = es(nc.sbuf_tensor("obfD", [128, 512], BF16))
        kvz_sb = es(nc.sbuf_tensor("kvzD", [128, 520], F32))
        s.op('gpsimd', lambda e: e.memset(onesb[:], 1.0))
        for k in range(P):
            s.dma(wq[:, k * 3 * D:(k + 1) * 3 * D], wq_in[k * 128:(k + 1) * 128, :])
            s.dma(gb[:, k:k + 1], g1_in[k * 128:(k + 1) * 128])
            s.dma(gb[:, P + k:P + k + 1], be1_in[k * 128:(k + 1) * 128])
            s.dma(qb_q[:, k:k + 1], qb_in[k * 128:(k + 1) * 128])
        s.dma(qrow[:], qb_in[:])
        for n4 in range(4):
            mm(s, ps[2][:, :], c['ones_row'][0:1, :], qrow[0:1, D + n4 * 512:D + (n4 + 1) * 512], True, True)
            s.op('scalar', lambda e, n4=n4: e.copy(out=bias_b[:, n4 * 512:(n4 + 1) * 512], in_=ps[2][:, :]))

        def eluplus(src_ap, dst_ap):
            s.op('scalar', lambda e, a=src_ap: e.activation(out=ev2[:], in_=a, func=AF.Relu))
            s.op('vector', lambda e, a=src_ap: e.tensor_scalar(out=ev3[:], in0=a, scalar1=0.0, scalar2=None, op0=ALU.min))
            s.op('scalar', lambda e: e.activation(out=ev3[:], in_=ev3[:], func=AF.Exp))
            s.op('vector', lambda e, d=dst_ap: e.tensor_tensor(out=d, in0=ev2[:], in1=ev3[:], op=ALU.add))

        for blk in range(NBLK):
            b0 = blk * 512
            for k in range(P):
                s.dma(x_t[k][:], x_in[k * 128:(k + 1) * 128, b0:b0 + 512])
            ln_T(s, c, [x_t[k][:] for k in range(P)], [src_t[k][:] for k in range(P)])
            ln_T(s, c, [src_t[k][:] for k in range(P)], [ln1[k][:] for k in range(P)], gain=gb[:, 0:P], bias=gb[:, P:2 * P])
            for m in range(P):
                for k in range(P):
                    mm(s, ps[2][:, :], wq[:, k * 3 * D + m * 128: k * 3 * D + m * 128 + 128], ln1[k][:], k == 0, k == P - 1)
                s.op('vector', lambda e, m=m: e.tensor_scalar(out=ev[:], in0=ps[2][:, :], scalar1=qb_q[:, m:m + 1], scalar2=None, op0=ALU.add))
                eluplus(ev[:], out_bf[:])
                s.dma(qfT_o[m * 128:(m + 1) * 128, b0:b0 + 512], out_bf[:])
            for tt in range(4):
                ti = blk * 4 + tt
                for half in range(2):
                    for n2 in range(2):
                        col0 = D + half * D + n2 * 512
                        for k in range(P):
                            mm(s, ps[3][:, :], ln1[k][:, tt * 128:tt * 128 + 128], wq[:, k * 3 * D + col0: k * 3 * D + col0 + 512], k == 0, k == P - 1)
                        s.op('vector', lambda e, half=half, n2=n2: e.tensor_tensor(out=ev[:], in0=ps[3][:, :], in1=bias_b[:, half * D + n2 * 512: half * D + (n2 + 1) * 512], op=ALU.add))
                        if half == 0:
                            eluplus(ev[:], kf_sb[:, ti * D + n2 * 512: ti * D + (n2 + 1) * 512])
                        else:
                            s.op('vector', lambda e, ti=ti, n2=n2: e.tensor_copy(out=v_sb[:, ti * D + n2 * 512: ti * D + (n2 + 1) * 512], in_=ev[:]))
        for hp in range(8):
            for sub in range(2):
                h = 2 * hp + sub
                pbank = ps[4 + sub]
                for ti in range(16):
                    mm(s, pbank[0:64, 0:64], kf_sb[:, ti * D + h * 64: ti * D + h * 64 + 64], v_sb[:, ti * D + h * 64: ti * D + h * 64 + 64], ti == 0, False)
                    mm(s, pbank[0:64, 64:65], kf_sb[:, ti * D + h * 64: ti * D + h * 64 + 64], onesb[:], ti == 0, ti == 15)
                s.op('scalar', lambda e, hp=hp, sub=sub, pb=pbank: e.copy(out=kvz_sb[sub * 64:sub * 64 + 64, hp * 65:(hp + 1) * 65], in_=pb[0:64, 0:65]))
        s.dma(kvz_o[:], kvz_sb[:])
        emit(nc, s)
    return nc


def build_E():
    nc = bass.Bass("TRN2", target_bir_lowering=False)
    s = Sched()
    x2_in = nc.dram_tensor("x2", [D, TPC], F32, kind="ExternalInput").ap()
    qfT_in = nc.dram_tensor("qfT", [D, TPC], BF16, kind="ExternalInput").ap()
    kvz_in = nc.dram_tensor("kvzp", [4, 128, 520], F32, kind="ExternalInput").ap()
    wo_in = nc.dram_tensor("outw", [D, D], BF16, kind="ExternalInput").ap()
    ob_in = nc.dram_tensor("outb", [D], F32, kind="ExternalInput").ap()
    w1_in = nc.dram_tensor("w1", [D, DFF], BF16, kind="ExternalInput").ap()
    b1_in = nc.dram_tensor("b1", [DFF], F32, kind="ExternalInput").ap()
    w2_in = nc.dram_tensor("w2", [DFF, D], BF16, kind="ExternalInput").ap()
    b2_in = nc.dram_tensor("b2", [D], F32, kind="ExternalInput").ap()
    g2_in = nc.dram_tensor("g2", [D], F32, kind="ExternalInput").ap()
    be2_in = nc.dram_tensor("be2", [D], F32, kind="ExternalInput").ap()
    x4_o = nc.dram_tensor("x4", [D, TPC], F32, kind="ExternalOutput").ap()
    NF = DFF // 128
    with ExitStack() as st:
        es = st.enter_context
        c = ln_common(nc, s, es)
        ps = c['ps']
        wo = es(nc.sbuf_tensor("woE", [128, P * D], BF16))
        w1 = es(nc.sbuf_tensor("w1E", [128, P * DFF], BF16))
        w2m = es(nc.sbuf_tensor("w2mE", [128, NF * 128], BF16))
        kvz = es(nc.sbuf_tensor("kvzE", [128, 520], F32))
        kvzb = es(nc.sbuf_tensor("kvzbE", [128, 520], BF16))
        kvp = es(nc.sbuf_tensor("kvpE", [128, 520], F32))
        gb = es(nc.sbuf_tensor("gbE", [128, 2 * P], F32))
        ob_sb = es(nc.sbuf_tensor("obE", [128, P], F32))
        b2_sb = es(nc.sbuf_tensor("b2E", [128, P], F32))
        b1_sb = es(nc.sbuf_tensor("b1E", [128, NF], F32))
        qf_t = [es(nc.sbuf_tensor(f"qfE{k}", [128, 512], BF16)) for k in range(P)]
        x2_t = [es(nc.sbuf_tensor(f"x2E{k}", [128, 512], F32)) for k in range(P)]
        xi_t = [es(nc.sbuf_tensor(f"xiE{k}", [128, 512], F32)) for k in range(P)]
        ln2 = [es(nc.sbuf_tensor(f"ln2E{k}", [128, 512], BF16)) for k in range(P)]
        h_bf = [es(nc.sbuf_tensor(f"hbfE{k}", [128, 512], BF16)) for k in range(NF)]
        rec = es(nc.sbuf_tensor("recE", [1, 512], F32))
        num_sb = es(nc.sbuf_tensor("numE", [64, 512], F32))
        yT = [es(nc.sbuf_tensor(f"yTE{k}", [128, 512], BF16)) for k in range(P)]
        ev = es(nc.sbuf_tensor("evE", [128, 512], F32))
        for k in range(P):
            s.dma(wo[:, k * D:(k + 1) * D], wo_in[k * 128:(k + 1) * 128, :])
            s.dma(w1[:, k * DFF:(k + 1) * DFF], w1_in[k * 128:(k + 1) * 128, :])
            s.dma(gb[:, k:k + 1], g2_in[k * 128:(k + 1) * 128])
            s.dma(gb[:, P + k:P + k + 1], be2_in[k * 128:(k + 1) * 128])
            s.dma(ob_sb[:, k:k + 1], ob_in[k * 128:(k + 1) * 128])
            s.dma(b2_sb[:, k:k + 1], b2_in[k * 128:(k + 1) * 128])
        for k in range(NF):
            s.dma(b1_sb[:, k:k + 1], b1_in[k * 128:(k + 1) * 128])
        s.dma(kvz[:], kvz_in[0, :, :])
        for r in range(1, 4):
            s.dma(kvp[:], kvz_in[r, :, :])
            s.op('vector', lambda e: e.tensor_tensor(out=kvz[:], in0=kvz[:], in1=kvp[:], op=ALU.add))
        s.op('scalar', lambda e: e.copy(out=kvzb[:], in_=kvz[:]))
        for blk in range(NBLK):
            b0 = blk * 512
            for k in range(P):
                s.dma(x2_t[k][:], x2_in[k * 128:(k + 1) * 128, b0:b0 + 512])
            for k in range(P):
                s.dma(qf_t[k][:], qfT_in[k * 128:(k + 1) * 128, b0:b0 + 512])
            ln_T(s, c, [x2_t[k][:] for k in range(P)], [xi_t[k][:] for k in range(P)])
            for h in range(H):
                hp, sub = h // 2, h % 2
                r0 = sub * 64
                qf_ap = qf_t[h // 2][(h % 2) * 64:(h % 2) * 64 + 64, :]
                mm(s, ps[2][0:64, :], kvzb[r0:r0 + 64, hp * 65:hp * 65 + 64], qf_ap, True, True)
                mm(s, ps[3][0:1, :], kvzb[r0:r0 + 64, hp * 65 + 64:hp * 65 + 65], qf_ap, True, True)
                s.op('vector', lambda e: e.tensor_scalar(out=rec[:], in0=ps[3][0:1, :], scalar1=1e-6, scalar2=None, op0=ALU.max))
                s.op('vector', lambda e: e.reciprocal(rec[:], rec[:]))
                mm(s, ps[4][0:64, :], c['ones_row'][0:1, 0:64], rec[0:1, :], True, True)
                s.op('scalar', lambda e: e.copy(out=num_sb[:], in_=ps[2][0:64, :]))
                s.op('vector', lambda e, h=h: e.tensor_tensor(out=yT[h // 2][(h % 2) * 64:(h % 2) * 64 + 64, :], in0=num_sb[:], in1=ps[4][0:64, :], op=ALU.mult))
            for m in range(P):
                for k in range(P):
                    mm(s, ps[2][:, :], wo[:, k * D + m * 128: k * D + m * 128 + 128], yT[k][:], k == 0, k == P - 1)
                s.op('vector', lambda e, m=m: e.tensor_scalar(out=ev[:], in0=ps[2][:, :], scalar1=ob_sb[:, m:m + 1], scalar2=None, op0=ALU.add))
                s.op('vector', lambda e, m=m: e.tensor_tensor(out=xi_t[m][:], in0=xi_t[m][:], in1=ev[:], op=ALU.add))
            ln_T(s, c, [xi_t[k][:] for k in range(P)], [ln2[k][:] for k in range(P)], gain=gb[:, 0:P], bias=gb[:, P:2 * P])
            for m in range(NF):
                for k in range(P):
                    mm(s, ps[2][:, :], w1[:, k * DFF + m * 128: k * DFF + m * 128 + 128], ln2[k][:], k == 0, k == P - 1)
                s.op('vector', lambda e, m=m: e.tensor_scalar(out=ev[:], in0=ps[2][:, :], scalar1=b1_sb[:, m:m + 1], scalar2=None, op0=ALU.add))
                s.op('scalar', lambda e, m=m: e.activation(out=h_bf[m][:], in_=ev[:], func=AF.Gelu))
            for m in range(P):
                for half in range(2):
                    hk0 = half * (DFF // 2)
                    s.dma(w2m[:, half * (NF // 2) * 128:(half + 1) * (NF // 2) * 128].rearrange("p (k c) -> p k c", c=128),
                          w2_in[hk0:hk0 + DFF // 2, m * 128:(m + 1) * 128].rearrange("(k p) c -> p k c", p=128))
                for k in range(NF):
                    mm(s, ps[2][:, :], w2m[:, k * 128:(k + 1) * 128], h_bf[k][:], k == 0, k == NF - 1)
                s.op('vector', lambda e, m=m: e.tensor_scalar(out=ev[:], in0=ps[2][:, :], scalar1=b2_sb[:, m:m + 1], scalar2=None, op0=ALU.add))
                s.op('vector', lambda e, m=m: e.tensor_tensor(out=ev[:], in0=ev[:], in1=xi_t[m][:], op=ALU.add))
                s.op('vector', lambda e, m=m: e.tensor_tensor(out=ev[:], in0=ev[:], in1=x2_t[m][:], op=ALU.add))
                s.dma(x4_o[m * 128:(m + 1) * 128, b0:b0 + 512], ev[:])
        emit(nc, s)
    return nc


def build_F():
    nc = bass.Bass("TRN2", target_bir_lowering=False)
    s = Sched()
    x4_in = nc.dram_tensor("x4", [D, TPC], F32, kind="ExternalInput").ap()
    wgT_in = nc.dram_tensor("wgT", [D, HID3], BF16, kind="ExternalInput").ap()
    wuT_in = nc.dram_tensor("wuT", [D, HID3], BF16, kind="ExternalInput").ap()
    wdT_in = nc.dram_tensor("wdT", [HID3, D], BF16, kind="ExternalInput").ap()
    id_in = nc.dram_tensor("ident", [128, 128], F32, kind="ExternalInput").ap()
    out_o = nc.dram_tensor("out", [TPC, D], F32, kind="ExternalOutput").ap()
    NK = HID3 // 128
    with ExitStack() as st:
        es = st.enter_context
        c = ln_common(nc, s, es)
        ps = c['ps']
        wg = es(nc.sbuf_tensor("wgF", [128, P * HID3], BF16))
        wu = es(nc.sbuf_tensor("wuF", [128, P * HID3], BF16))
        wdk = es(nc.sbuf_tensor("wdkF", [128, 512], BF16))
        ident = es(nc.sbuf_tensor("identF", [128, 128], F32))
        x_t = [es(nc.sbuf_tensor(f"xtF{k}", [128, 512], F32)) for k in range(P)]
        lnT = [es(nc.sbuf_tensor(f"lnF{k}", [128, 512], BF16)) for k in range(P)]
        h1 = es(nc.sbuf_tensor("h1F", [128, 512], BF16))
        h_bf = [es(nc.sbuf_tensor(f"hbfF{k}", [128, 512], BF16)) for k in range(NK)]
        xn = es(nc.sbuf_tensor("xnF", [128, 512], F32))
        ev = es(nc.sbuf_tensor("evF", [128, 512], F32))
        s.dma(ident[:], id_in[:])
        for k in range(P):
            s.dma(wg[:, k * HID3:(k + 1) * HID3], wgT_in[k * 128:(k + 1) * 128, :])
            s.dma(wu[:, k * HID3:(k + 1) * HID3], wuT_in[k * 128:(k + 1) * 128, :])
        for blk in range(NBLK):
            for k in range(P):
                s.dma(x_t[k][:], x4_in[k * 128:(k + 1) * 128, blk * 512:blk * 512 + 512])
            ln_T(s, c, [x_t[k][:] for k in range(P)], [lnT[k][:] for k in range(P)])
            for m in range(NK):
                for k in range(P):
                    mm(s, ps[2][:, :], wg[:, k * HID3 + m * 128: k * HID3 + m * 128 + 128], lnT[k][:], k == 0, k == P - 1)
                s.op('scalar', lambda e: e.activation(out=h1[:], in_=ps[2][:, :], func=AF.Silu))
                for k in range(P):
                    mm(s, ps[3][:, :], wu[:, k * HID3 + m * 128: k * HID3 + m * 128 + 128], lnT[k][:], k == 0, k == P - 1)
                s.op('vector', lambda e, m=m: e.tensor_tensor(out=h_bf[m][:], in0=h1[:], in1=ps[3][:, :], op=ALU.mult))
            for n2 in range(2):
                for k in range(NK):
                    s.dma(wdk[:], wdT_in[k * 128:(k + 1) * 128, n2 * 512:(n2 + 1) * 512])
                    for tt in range(4):
                        mm(s, ps[2 + tt][:, :], h_bf[k][:, tt * 128:tt * 128 + 128], wdk[:], k == 0, k == NK - 1)
                for tt in range(4):
                    t0 = (blk * 4 + tt) * 128
                    for q in range(4):
                        dchunk = n2 * 4 + q
                        s.op('tensor', lambda e, dchunk=dchunk, tt=tt, q=q: e.transpose(ps[6][:, q * 128:(q + 1) * 128], x_t[dchunk][:, tt * 128:tt * 128 + 128], ident[:]))
                    s.op('scalar', lambda e: e.copy(out=xn[:], in_=ps[6][:, :]))
                    s.op('vector', lambda e, tt=tt: e.tensor_tensor(out=ev[:], in0=xn[:], in1=ps[2 + tt][:, :], op=ALU.add))
                    s.dma(out_o[t0:t0 + 128, n2 * 512:(n2 + 1) * 512], ev[:])
        emit(nc, s)
    return nc


def _bf(a):
    return np.asarray(a).astype(BF)


def _rope_tables():
    inv = 1.0 / (10000.0 ** (np.arange(0, DH, 2, dtype=np.float64) / DH))
    pos = np.arange(T, dtype=np.float64)
    fr = pos[:, None] * inv[None, :]
    cos = np.cos(fr).astype(np.float32).T
    sin = np.sin(fr).astype(np.float32).T
    Cf = np.tile(cos, (4, 1))
    Sf = np.concatenate([-sin, sin, -sin, sin], axis=0)
    sc = np.float32(1.0 / np.sqrt(DH))
    return Cf, Sf, (Cf * sc).astype(np.float32), (Sf * sc).astype(np.float32)


def _masks():
    ones = np.ones((128, 512), np.float32)
    zeros = np.zeros((128, 512), np.float32)
    f = np.arange(512)[None, :]
    p = np.arange(128)[:, None]
    diags = [(r * 128 + p <= f).astype(np.float32) for r in range(4)]
    out = []
    for cidx in range(NCORES):
        tiles = []
        for j in range(4):
            m = 8 * j + cidx
            i = 15 - m // 2
            for n in range(FJ[j]):
                if n < 4 * i:
                    tiles.append(ones)
                elif n < 4 * (i + 1):
                    tiles.append(diags[n - 4 * i])
                else:
                    tiles.append(zeros)
        out.append(_bf(np.concatenate(tiles, axis=1)))
    return out


def _block_of(cidx, j):
    m = 8 * j + cidx
    return (m % 2, 15 - m // 2)


_cache = {}


def _get(name, fn):
    if name not in _cache:
        _cache[name] = fn()
    return _cache[name]


def kernel(**inp):
    cores = list(range(NCORES))
    ident = np.eye(128, dtype=np.float32)
    identb = _bf(ident)

    srcs = {
        "wqkv": inp['L0_attn_Wqkv'], "wout": inp['L0_attn_Wout'],
        "sw1": inp['L0_ffn_sw1'], "sw2": inp['L0_ffn_sw2'], "sw3": inp['L0_ffn_sw3'],
        "ew1": np.asarray(inp['L0_ffn_ew1']).reshape(8192, 2048),
        "ew2": np.asarray(inp['L0_ffn_ew2']).reshape(8192, 2048),
        "ew3": np.asarray(inp['L0_ffn_ew3']).reshape(16384, 1024),
        "rw": inp['L0_ffn_rw'],
        "qkvw": inp['L1_attn_qkv_w'], "outw": inp['L1_attn_out_w'],
        "w1": inp['L1_attn_w1'], "w2": inp['L1_attn_w2'],
        "wgT": np.ascontiguousarray(np.asarray(inp['L1_ffn_w_gate']).T),
        "wuT": np.ascontiguousarray(np.asarray(inp['L1_ffn_w_up']).T),
        "wdT": np.ascontiguousarray(np.asarray(inp['L1_ffn_w_down']).T),
    }
    ncP = _get('P', build_prep)
    in_maps = []
    for ci in cores:
        m = {}
        for (name, rows, cols, sdt) in PREP_JOBS:
            m[name + "_in"] = np.ascontiguousarray(np.asarray(srcs[name]).reshape(-1, cols)[ci * rows:(ci + 1) * rows])
        in_maps.append(m)
    resP = run_bass_kernel_spmd(ncP, in_maps, core_ids=cores)
    W = {}
    for (name, rows, cols, sdt) in PREP_JOBS:
        W[name] = np.concatenate([resP.results[ci][name + "_out"] for ci in cores], axis=0)

    Cf, Sf, Cq, Sq = _get('rope', _rope_tables)
    x_flat = np.asarray(inp['L0_attn_x']).reshape(NT, D)
    ncA = _get('A', build_A)
    in_maps = []
    for ci in cores:
        pos0 = (ci % 4) * TPC
        in_maps.append({
            "x": np.ascontiguousarray(x_flat[ci * TPC:(ci + 1) * TPC]),
            "wqkv": W['wqkv'],
            "cq": np.ascontiguousarray(Cq[:, pos0:pos0 + TPC]),
            "sq": np.ascontiguousarray(Sq[:, pos0:pos0 + TPC]),
            "ck": np.ascontiguousarray(Cf[:, pos0:pos0 + TPC]),
            "sk": np.ascontiguousarray(Sf[:, pos0:pos0 + TPC]),
            "ident": ident, "identb": identb,
        })
    resA = run_bass_kernel_spmd(ncA, in_maps, core_ids=cores)
    xT0 = np.concatenate([resA.results[ci]["xT"] for ci in cores], axis=1)
    qT = np.concatenate([resA.results[ci]["qT"] for ci in cores], axis=1)
    kT = np.concatenate([resA.results[ci]["kT"] for ci in cores], axis=1)
    vfull = [np.concatenate([resA.results[ci]["v"] for ci in range(4 * b, 4 * b + 4)], axis=1) for b in range(2)]

    masks = _get('masks', _masks)
    ncB = _get('B', build_B)
    in_maps = []
    for ci in cores:
        m = {"masks": masks[ci], "wout": W['wout']}
        qcols, x0cols = [], []
        for j in range(4):
            b, i = _block_of(ci, j)
            c0 = b * T + i * 512
            qcols.append(qT[:, c0:c0 + 512])
            x0cols.append(xT0[:, c0:c0 + 512])
            F = FJ[j]
            m[f"kT{j}"] = np.ascontiguousarray(kT[:, b * T:b * T + F * 128])
            vv = vfull[b][:, :F * 128, :].reshape(H, F, 128, DH).transpose(0, 2, 1, 3)
            vv = np.concatenate([vv, np.ones((H, 128, F, 1), BF)], axis=3)
            m[f"v{j}"] = np.ascontiguousarray(vv.reshape(H, 128, F * 65))
        m["qT"] = np.ascontiguousarray(np.concatenate(qcols, axis=1))
        m["x0"] = np.ascontiguousarray(np.concatenate(x0cols, axis=1))
        in_maps.append(m)
    resB = run_bass_kernel_spmd(ncB, in_maps, core_ids=cores)
    xT1 = np.empty((D, NT), np.float32)
    for ci in cores:
        for j in range(4):
            b, i = _block_of(ci, j)
            xT1[:, b * T + i * 512: b * T + (i + 1) * 512] = resB.results[ci]["xT1"][:, j * 512:(j + 1) * 512]

    ncC = _get('C', build_C)
    in_maps = [{
        "x1": np.ascontiguousarray(xT1[:, ci * TPC:(ci + 1) * TPC]),
        "sw1": W['sw1'], "sw2": W['sw2'], "sw3": W['sw3'],
        "ew1": W['ew1'], "ew2": W['ew2'], "ew3": W['ew3'],
        "rw": W['rw'], "rb": np.ascontiguousarray(np.asarray(inp['L0_ffn_rb'])),
        "ident": ident,
    } for ci in cores]
    resC = run_bass_kernel_spmd(ncC, in_maps, core_ids=cores)
    xT2 = np.concatenate([resC.results[ci]["x2"] for ci in cores], axis=1)

    ncD = _get('D', build_D)
    in_maps = [{
        "x2": np.ascontiguousarray(xT2[:, ci * TPC:(ci + 1) * TPC]),
        "qkvw": W['qkvw'], "qkvb": np.ascontiguousarray(np.asarray(inp['L1_attn_qkv_b'])),
        "g1": np.ascontiguousarray(np.asarray(inp['L1_attn_g1'])),
        "be1": np.ascontiguousarray(np.asarray(inp['L1_attn_be1'])),
    } for ci in cores]
    resD = run_bass_kernel_spmd(ncD, in_maps, core_ids=cores)
    qfT = np.concatenate([resD.results[ci]["qfT"] for ci in cores], axis=1)
    kvz = np.stack([resD.results[ci]["kvz"] for ci in cores])

    ncE = _get('E', build_E)
    in_maps = [{
        "x2": np.ascontiguousarray(xT2[:, ci * TPC:(ci + 1) * TPC]),
        "qfT": np.ascontiguousarray(qfT[:, ci * TPC:(ci + 1) * TPC]),
        "kvzp": np.ascontiguousarray(kvz[4 * (ci // 4):4 * (ci // 4) + 4]),
        "outw": W['outw'], "outb": np.ascontiguousarray(np.asarray(inp['L1_attn_out_b'])),
        "w1": W['w1'], "b1": np.ascontiguousarray(np.asarray(inp['L1_attn_b1'])),
        "w2": W['w2'], "b2": np.ascontiguousarray(np.asarray(inp['L1_attn_b2'])),
        "g2": np.ascontiguousarray(np.asarray(inp['L1_attn_g2'])),
        "be2": np.ascontiguousarray(np.asarray(inp['L1_attn_be2'])),
    } for ci in cores]
    resE = run_bass_kernel_spmd(ncE, in_maps, core_ids=cores)
    xT4 = np.concatenate([resE.results[ci]["x4"] for ci in cores], axis=1)

    ncF = _get('F', build_F)
    in_maps = [{
        "x4": np.ascontiguousarray(xT4[:, ci * TPC:(ci + 1) * TPC]),
        "wgT": W['wgT'], "wuT": W['wuT'], "wdT": W['wdT'],
        "ident": ident,
    } for ci in cores]
    resF = run_bass_kernel_spmd(ncF, in_maps, core_ids=cores)
    out = np.concatenate([resF.results[ci]["out"] for ci in cores], axis=0)
    return np.ascontiguousarray(out.reshape(B, T, D).astype(np.float32))
